# version 05
# baseline (speedup 1.0000x reference)
import numpy as np

import concourse.bass as bass
import concourse.mybir as mybir
from concourse.bass_utils import run_bass_kernel_spmd

F32 = mybir.dt.float32
F16 = mybir.dt.float16
AF = mybir.ActivationFunctionType
ALU = mybir.AluOpType
AX = mybir.AxisListType.X

D = 2048
T = 2048
NCH = 16
TCW = 512
NTC = 4
NH = 16
HD = 128
FH = 1408
NHT = 11
SH = 2816
NSHT = 22
NE = 16
INV_D = 1.0 / 2048.0
QSC = float(1.0 / np.sqrt(128.0))
EXPB = -4.0

ACOL = 14336
HBC = 0
CB = 12288
AQ = 0
AKK = 2048
AVV = 4096
APT = 6144
ARS = 7168
AEV = 7680
AWO = 0
ABO = 8192
LRB = 0
LSQ0, LSQ1 = 2048, 4096
LM, LV, LR = 6144, 8192, 10240
CST_MASK = 0
CST_IDEN = 2048
CST_ONEC = 2047


class Sem:
    def __init__(self, nc, name):
        self.h = nc.alloc_semaphore(name)
        self.n = 0


class Sched:
    ENGS = ("tensor", "vector", "scalar", "gpsimd", "sync")

    def __init__(self):
        self.q = {e: [] for e in self.ENGS}
        self.primary = {}

    def emit(self, eng, fn, waits=(), inc=None, inc16=None):
        ws = [w for w in waits if w is not None]
        psem = self.primary.get(eng)
        if psem is not None:
            if psem.n:
                ws.append((psem, psem.n))
            if inc is None:
                inc = psem
            assert inc is psem, "vector/scalar ops must inc their primary sem"
        incs = []
        ret = None
        if inc is not None:
            inc.n += 1
            ret = inc.n
            incs.append((inc, 1))
        if inc16 is not None:
            inc16.n += 16
            ret = inc16.n
            incs.append((inc16, 16))
        self.q[eng].append((ws, fn, incs))
        return ret

    def drain(self, block):
        def mk(name):
            def body(e):
                seen = {}
                for waits, fn, incs in self.q[name]:
                    ws = []
                    for sem, val in waits:
                        if val > seen.get(id(sem), 0):
                            ws.append((sem, val))
                            seen[id(sem)] = val
                    for sem, val in ws[:-1]:
                        e.wait_ge(sem.h, val)
                    ins = fn(e)
                    if ws:
                        ins._wait_ge(ws[-1][0].h, ws[-1][1])
                    for sem, k in incs:
                        ins = ins.then_inc(sem.h, k)
            return body
        block.tensor(mk("tensor"))
        block.vector(mk("vector"))
        block.scalar(mk("scalar"))
        block.gpsimd(mk("gpsimd"))
        block.sync(mk("sync"))


class Banks:
    def __init__(self, tiles):
        self.t = tiles
        self.rel = [None] * len(tiles)
        self.busy = [False] * len(tiles)
        self.rr = 0

    def acquire(self):
        for k in range(len(self.t)):
            i = (self.rr + k) % len(self.t)
            if not self.busy[i]:
                self.rr = (i + 1) % len(self.t)
                self.busy[i] = True
                w = self.rel[i]
                return i, self.t[i], ([w] if w else [])
        raise RuntimeError("psum banks exhausted")

    def release(self, i, sem, val):
        self.busy[i] = False
        self.rel[i] = (sem, val)


class Slots:
    def __init__(self, ten, n, width):
        self.ten = ten
        self.n = n
        self.w = width
        self.rel = [None] * n
        self.rr = 0

    def acquire(self):
        i = self.rr
        self.rr = (i + 1) % self.n
        w = self.rel[i]
        return i, self.ten[:, i * self.w:(i + 1) * self.w], ([w] if w else [])

    def release(self, i, sem, val):
        self.rel[i] = (sem, val)


class DmaQ:

    def __init__(self, nc, sd, eng, name, nlanes):
        self.sd = sd
        self.eng = eng
        self.lanes = [Sem(nc, f"{name}{i}") for i in range(nlanes)]
        self.i = 0

    def dma(self, fn, waits=()):
        lane = self.lanes[self.i % len(self.lanes)]
        self.i += 1
        w = [x for x in waits if x is not None]
        if lane.n:
            w.append((lane, lane.n))
        rdy = self.sd.emit(self.eng, fn, waits=w, inc16=lane)
        return (lane, rdy)

    def barrier(self):
        return [(l, l.n) for l in self.lanes if l.n]


def build_program(n_sub=4):
    nc = bass.Bass("TRN2", target_bir_lowering=False)

    dxin = nc.dram_tensor("xin", [128, NCH * T], F32, kind="ExternalInput")
    dout = nc.dram_tensor("out", [NCH * 128, T], F32, kind="ExternalOutput")
    dwq_a = nc.dram_tensor("wq_a", [NH * 128, D], F32, kind="ExternalInput")
    dwk_a = nc.dram_tensor("wk_a", [4 * 128, D], F32, kind="ExternalInput")
    dwv_a = nc.dram_tensor("wv_a", [4 * 128, D], F32, kind="ExternalInput")
    dbq_a = nc.dram_tensor("bq_a", [1, NH * HD], F32, kind="ExternalInput")
    dbk_a = nc.dram_tensor("bk_a", [1, 4 * HD], F32, kind="ExternalInput")
    dbv_a = nc.dram_tensor("bv_a", [1, 4 * HD], F32, kind="ExternalInput")
    dwq_b = nc.dram_tensor("wq_b", [NH * 128, D], F32, kind="ExternalInput")
    dwk_b = nc.dram_tensor("wk_b", [NH * 128, D], F32, kind="ExternalInput")
    dwv_b = nc.dram_tensor("wv_b", [NH * 128, D], F32, kind="ExternalInput")
    dwo_b = nc.dram_tensor("wo_b", [NH * 128, D], F32, kind="ExternalInput")
    dbq_b = nc.dram_tensor("bq_b", [1, D], F32, kind="ExternalInput")
    dbk_b = nc.dram_tensor("bk_b", [1, D], F32, kind="ExternalInput")
    dbv_b = nc.dram_tensor("bv_b", [1, D], F32, kind="ExternalInput")
    dbo_b = nc.dram_tensor("bo_b", [1, D], F32, kind="ExternalInput")
    dgw = nc.dram_tensor("gw", [128, NCH * NE], F32, kind="ExternalInput")
    dgb = nc.dram_tensor("gb", [1, NE], F32, kind="ExternalInput")
    dw1 = nc.dram_tensor("w1", [NE * NHT * 128, D], F32, kind="ExternalInput")
    dw3 = nc.dram_tensor("w3", [NE * NHT * 128, D], F32, kind="ExternalInput")
    dw2 = nc.dram_tensor("w2", [NE * NHT * 8 * 128, 256], F32, kind="ExternalInput")
    db1 = nc.dram_tensor("b1", [NE, FH], F32, kind="ExternalInput")
    db3 = nc.dram_tensor("b3", [NE, FH], F32, kind="ExternalInput")
    db2 = nc.dram_tensor("b2", [NE, D], F32, kind="ExternalInput")
    dws1 = nc.dram_tensor("ws1", [NSHT * 128, D], F32, kind="ExternalInput")
    dws3 = nc.dram_tensor("ws3", [NSHT * 128, D], F32, kind="ExternalInput")
    dws2 = nc.dram_tensor("ws2", [NSHT * 8 * 128, 256], F32, kind="ExternalInput")
    dbs1 = nc.dram_tensor("bs1", [1, SH], F32, kind="ExternalInput")
    dbs3 = nc.dram_tensor("bs3", [1, SH], F32, kind="ExternalInput")
    dbs2 = nc.dram_tensor("bs2", [1, D], F32, kind="ExternalInput")
    dcst = nc.dram_tensor("cst", [128, 2176], F16, kind="ExternalInput")
    dcfl = nc.dram_tensor("cfl", [128, 2], F32, kind="ExternalInput")

    dattn = nc.dram_tensor("attnout", [NTC * 128, NH * TCW], F16, kind="Internal")
    dcw1 = nc.dram_tensor("cache_w1", [NE * 2, FH], F16, kind="Internal")
    dcw3 = nc.dram_tensor("cache_w3", [NE * 2, FH], F16, kind="Internal")
    dcs = nc.dram_tensor("cache_s", [4, SH], F16, kind="Internal")

    xb = nc.alloc_sbuf_tensor("xb", [128, NCH * T], F16)
    xr = nc.alloc_sbuf_tensor("xr", [128, NCH * T], F16)
    poolA_t = nc.alloc_sbuf_tensor("poolA", [128, 3 * 2048], F16)
    poolB_t = nc.alloc_sbuf_tensor("poolB", [128, 12 * 256], F16)
    arena = nc.alloc_sbuf_tensor("arena", [128, ACOL], F16)
    fscr = nc.alloc_sbuf_tensor("fscr", [128, 1024], F32)
    gsc = nc.alloc_sbuf_tensor("gsc", [128, 88], F32)
    cst = nc.alloc_sbuf_tensor("cstb", [128, 2176], F16)
    mr2 = nc.alloc_sbuf_tensor("mr2", [2, T], F16)
    corr = nc.alloc_sbuf_tensor("corr", [2, 6144], F16)
    gwb = nc.alloc_sbuf_tensor("gwb", [128, NCH * NE], F16)
    gsum = nc.alloc_sbuf_tensor("gsum", [1, NE], F16)
    gbr = nc.alloc_sbuf_tensor("gbr", [1, NE], F16)
    gbb = nc.alloc_sbuf_tensor("gbb", [128, NE], F16)
    b2b = nc.alloc_sbuf_tensor("b2b", [NE, D], F16)
    cwT = nc.alloc_sbuf_tensor("cwT", [NE, T], F16)
    cwf = nc.alloc_sbuf_tensor("cwf", [128, NE], F16)
    cfl = nc.alloc_sbuf_tensor("cflb", [128, 2], F32)
    cflh = nc.alloc_sbuf_tensor("cflhb", [128, 2], F16)
    nc.const_aps.register(F32, 1e-5, cfl[:, 0:1])
    nc.const_aps.register(F32, EXPB, cfl[:, 1:2])
    nc.const_aps.register(F16, 1e-5, cflh[:, 0:1])
    nc.const_aps.register(F16, EXPB, cflh[:, 1:2])
    assert nc.sbuf_bytes_remaining >= 0, f"SBUF over by {-nc.sbuf_bytes_remaining} B"

    psf = [nc.alloc_psum_tensor(f"ps{i}", [128, 512], F32) for i in range(7)]
    ps16 = nc.alloc_psum_tensor("ps16", [128, 1024], F16)
    bk = Banks(psf)
    poolA = Slots(poolA_t, 3, 2048)
    poolB = Slots(poolB_t, 12, 256)

    sd = Sched()
    s_pe = Sem(nc, "s_pe")
    s_dv = Sem(nc, "s_dv")
    s_ac = Sem(nc, "s_ac")
    sd.primary["vector"] = s_dv
    sd.primary["scalar"] = s_ac
    gq = DmaQ(nc, sd, "gpsimd", "gl", 8)
    sq = DmaQ(nc, sd, "sync", "sl", 6)

    onescol = cst[:, CST_ONEC:CST_ONEC + 1]

    def ones_lhsT(m=128):
        return cst[0:1, 0:m]

    def onesrow(n=TCW):
        return cst[0:1, 0:n]

    def gd(dst, src):
        return lambda e, dst=dst, src=src: e.dma_start(dst, src)

    corr_guard = [[]]
    crow_busy = [None]
    ri_busy = [None]
    aev_busy = [None, None]
    s1_busy = [None, None]
    C_cons = [None]
    cwtr_rel = [None] * 8
    cache_wr = {}
    Cpend = {}
    xfin = [None] * NCH

    X_RDY = gq.dma(gd(xb[:, :], dxin[:, :]))
    CST_W = [gq.dma(gd(cst[:, :], dcst[:, :])),
             gq.dma(gd(gwb[:, :], dgw[:, :])),
             gq.dma(gd(gbr[:, :], dgb[:, :])),
             gq.dma(gd(cfl[:, :], dcfl[:, :])),
             gq.dma(gd(cflh[:, :], dcfl[:, :])),
             gq.dma(gd(b2b[:, :], db2[:, :]))]
    MR1_W = [sq.dma(gd(mr2[1:2, p * 512:(p + 1) * 512], dcst[0:1, 0:512]))
             for p in range(4)]

    bi, pt, w = bk.acquire()
    vv = None
    for c in range(NCH):
        vv = sd.emit("tensor",
                     lambda t, pt=pt, c=c: t.matmul(pt[0:1, 0:NE], onescol,
                                                    gwb[:, c * NE:(c + 1) * NE],
                                                    start=(c == 0), stop=(c == NCH - 1)),
                     waits=(w + CST_W) if c == 0 else [],
                     inc=s_pe if c == NCH - 1 else None)
    GS = sd.emit("scalar", lambda s, pt=pt: s.mul(gsum[:, :], pt[0:1, 0:NE], -1.0),
                 waits=[(s_pe, vv)], inc=s_ac)
    bk.release(bi, s_ac, GS)
    bi, pt, w = bk.acquire()
    vv = sd.emit("tensor", lambda t, pt=pt: t.matmul(pt[:, 0:NE], ones_lhsT(), gbr[0:1, :],
                                                     start=True, stop=True),
                 waits=w + CST_W, inc=s_pe)
    GB = sd.emit("scalar", lambda s, pt=pt: s.copy(gbb[:, :], pt[:, 0:NE]),
                 waits=[(s_pe, vv)], inc=s_ac)
    bk.release(bi, s_ac, GB)
    GATE_RDY = GB

    def emit_ln():
        bw = [(s_pe, s_pe.n), (s_dv, s_dv.n), (s_ac, s_ac.n), X_RDY] + MR1_W + CST_W
        sbk = [bk.acquire() for _ in range(4)]
        SUM_D = None
        for c in range(NCH):
            for tc in range(NTC):
                bi, pt, w = sbk[tc]
                SUM_D = sd.emit("tensor",
                                lambda t, pt=pt, c=c, tc=tc: t.matmul(
                                    pt[0:1, 0:TCW], onescol,
                                    xb[:, c * T + tc * TCW:c * T + (tc + 1) * TCW],
                                    start=(c == 0), stop=(c == NCH - 1)),
                                waits=[xfin[c], X_RDY] + (w + MR1_W + CST_W
                                                          if c == 0 else []),
                                inc=s_pe if c == NCH - 1 else None) or SUM_D
        M_D = None
        for tc in range(NTC):
            bi, pt, _ = sbk[tc]
            M_D = sd.emit("scalar", lambda s, pt=pt, tc=tc: s.mul(
                arena[0:1, LM + tc * TCW:LM + (tc + 1) * TCW], pt[0:1, 0:TCW], INV_D),
                waits=[(s_pe, SUM_D)], inc=s_ac)
            bk.release(bi, s_ac, M_D)
        qbk = [bk.acquire() for _ in range(4)]
        sqrel = [None, None]
        SQ_D = None
        for c in range(NCH):
            slot = LSQ0 if c % 2 == 0 else LSQ1
            SQ = sd.emit("scalar", lambda s, c=c, slot=slot: s.activation(
                arena[:, slot:slot + T], xb[:, c * T:(c + 1) * T], AF.Square),
                waits=bw + [xfin[c]] + ([sqrel[c % 2]] if sqrel[c % 2] else []),
                inc=s_ac)
            for tc in range(NTC):
                bi, pt, w = qbk[tc]
                SQ_D = sd.emit("tensor", lambda t, pt=pt, slot=slot, tc=tc, c=c: t.matmul(
                    pt[0:1, 0:TCW], onescol,
                    arena[:, slot + tc * TCW:slot + (tc + 1) * TCW],
                    start=(c == 0), stop=(c == NCH - 1)),
                    waits=[(s_ac, SQ)] + (w if c == 0 else []),
                    inc=s_pe if tc == NTC - 1 else None) or SQ_D
            sqrel[c % 2] = (s_pe, s_pe.n)
        MSQ = sd.emit("vector", lambda v: v.tensor_mul(
            arena[0:1, LV:LV + T], arena[0:1, LM:LM + T], arena[0:1, LM:LM + T]),
            waits=[(s_ac, M_D)], inc=s_dv)
        VAR_D = None
        for tc in range(NTC):
            bi, pt, _ = qbk[tc]
            VAR_D = sd.emit("vector", lambda v, pt=pt, tc=tc: v.scalar_tensor_tensor(
                arena[0:1, LV + tc * TCW:LV + (tc + 1) * TCW], pt[0:1, 0:TCW], INV_D,
                arena[0:1, LV + tc * TCW:LV + (tc + 1) * TCW], ALU.mult, ALU.subtract),
                waits=[(s_pe, SQ_D), (s_dv, MSQ)], inc=s_dv)
            bk.release(bi, s_dv, VAR_D)
        sd.emit("scalar", lambda s: s.activation(
            arena[0:1, LR:LR + T], arena[0:1, LV:LV + T], AF.Ln, bias=1e-5),
            waits=[(s_dv, VAR_D)], inc=s_ac)
        R_D = sd.emit("scalar", lambda s: s.activation(
            arena[0:1, LV:LV + T], arena[0:1, LR:LR + T], AF.Exp, scale=-0.5), inc=s_ac)
        MR_D = sd.emit("vector", lambda v: v.tensor_mul(
            mr2[0:1, :], arena[0:1, LM:LM + T], arena[0:1, LV:LV + T]),
            waits=[(s_ac, R_D)] + MR1_W, inc=s_dv)
        bi, pt, w = bk.acquire()
        RB_D = None
        for tc in range(NTC):
            v = sd.emit("tensor", lambda t, pt=pt, tc=tc: t.matmul(
                pt[:, 0:TCW], ones_lhsT(), arena[0:1, LV + tc * TCW:LV + (tc + 1) * TCW],
                start=True, stop=True),
                waits=(w + [(s_ac, R_D)]) if tc == 0 else [(s_ac, RB_D)], inc=s_pe)
            RB_D = sd.emit("scalar", lambda s, pt=pt, tc=tc: s.copy(
                arena[:, LRB + tc * TCW:LRB + (tc + 1) * TCW], pt[:, 0:TCW]),
                waits=[(s_pe, v)], inc=s_ac)
        bk.release(bi, s_ac, RB_D)
        xr_rdy = []
        for c in range(NCH):
            xr_rdy.append(sd.emit("vector", lambda v, c=c: v.tensor_mul(
                xr[:, c * T:(c + 1) * T], xb[:, c * T:(c + 1) * T], arena[:, LRB:LRB + T]),
                waits=[(s_ac, RB_D)], inc=s_dv))
        return xr_rdy, MR_D

    def load_slabA(dram, row0, waits=()):
        i, view, w = poolA.acquire()
        rdy = gq.dma(gd(view, dram[row0:row0 + 128, :]), waits=list(w) + list(waits))
        return i, view, rdy

    def load_slabB(dram, row0, waits=()):
        i, view, w = poolB.acquire()
        rdy = gq.dma(gd(view, dram[row0:row0 + 128, :]), waits=list(w) + list(waits))
        return i, view, rdy

    def emit_wsum(view, rdy, corr_off):
        bi, pt, w = bk.acquire()
        vv = None
        for c in range(NCH):
            vv = sd.emit("tensor", lambda t, pt=pt, view=view, c=c: t.matmul(
                pt[0:1, 0:128], onescol, view[:, c * 128:(c + 1) * 128],
                start=(c == 0), stop=(c == NCH - 1)),
                waits=(w + [rdy] + corr_guard[0]) if c == 0 else [],
                inc=s_pe if c == NCH - 1 else None) or vv
        r = sd.emit("scalar", lambda s, pt=pt, corr_off=corr_off: s.mul(
            corr[0:1, corr_off:corr_off + 128], pt[0:1, 0:128], -1.0),
            waits=[(s_pe, vv)], inc=s_ac)
        bk.release(bi, s_ac, r)
        return r

    def emit_attn(kind, XR_D, MR_D):
        CQ, CK, CV = 0, 2048, 4096
        if kind == "a":
            wq, wk, wv, bqd, bkd, bvd = dwq_a, dwk_a, dwv_a, dbq_a, dbk_a, dbv_a
            ngrp, hpg = 4, 4
        else:
            wq, wk, wv, bqd, bkd, bvd = dwq_b, dwk_b, dwv_b, dbq_b, dbk_b, dbv_b
            ngrp, hpg = 16, 1
        BIAS_W = [gq.dma(gd(corr[1:2, CQ:CQ + NH * HD], bqd[0:1, :]),
                         waits=corr_guard[0]),
                  gq.dma(gd(corr[1:2, CK:CK + ngrp * HD], bkd[0:1, :]),
                         waits=corr_guard[0]),
                  gq.dma(gd(corr[1:2, CV:CV + ngrp * HD], bvd[0:1, :]),
                         waits=corr_guard[0])]
        base = [(s_dv, MR_D)] + BIAS_W + CST_W
        tail = [None]

        def proj_fm(view, wrdy, outcol, corr_seg, scale, extra):
            lastac = lastpe = None
            for tc in range(NTC):
                bi, pt, w = bk.acquire()
                for c in range(NCH):
                    sd.emit("tensor", lambda t, pt=pt, view=view, c=c, tc=tc: t.matmul(
                        pt[:, 0:TCW], view[:, c * 128:(c + 1) * 128],
                        xr[:, c * T + tc * TCW:c * T + (tc + 1) * TCW],
                        start=(c == 0), stop=False),
                        waits=([(s_dv, XR_D[c])] + ((w + [wrdy] + base + extra)
                               if c == 0 else [])))
                lastpe = sd.emit("tensor", lambda t, pt=pt, corr_seg=corr_seg, tc=tc: t.matmul(
                    pt[:, 0:TCW], corr[:, corr_seg:corr_seg + 128],
                    mr2[:, tc * TCW:(tc + 1) * TCW], start=False, stop=True), inc=s_pe)
                lastac = sd.emit("scalar", lambda s, pt=pt, outcol=outcol, tc=tc,
                                 scale=scale: s.mul(
                    arena[:, outcol + tc * TCW:outcol + (tc + 1) * TCW], pt[:, 0:TCW], scale),
                    waits=[(s_pe, lastpe)], inc=s_ac)
                bk.release(bi, s_ac, lastac)
            return lastac, lastpe

        for g in range(ngrp):
            ki, kview, krdy = load_slabA(wk, g * 128)
            ksum = emit_wsum(kview, krdy, CK + g * 128)
            K_D, kpe = proj_fm(kview, krdy, AKK, CK + g * 128, 1.0, [(s_ac, ksum)])
            poolA.release(ki, s_pe, kpe)
            vi, vview, vrdy = load_slabA(wv, g * 128)
            vsum = emit_wsum(vview, vrdy, CV + g * 128)
            V_D = vpe = None
            for kt4 in range(4):
                bi, pt, w = bk.acquire()
                for sub in range(4):
                    kt = kt4 * 4 + sub
                    for c in range(NCH):
                        sd.emit("tensor",
                                lambda t, pt=pt, vw=vview, c=c, kt=kt, sub=sub: t.matmul(
                                    pt[:, sub * 128:(sub + 1) * 128],
                                    xr[:, c * T + kt * 128:c * T + kt * 128 + 128],
                                    vw[:, c * 128:(c + 1) * 128],
                                    start=(c == 0), stop=False),
                                waits=([(s_dv, XR_D[c])] + (w + [vrdy, (s_ac, vsum)] + base
                                       if (sub == 0 and c == 0) else [])))
                    vpe = sd.emit("tensor", lambda t, pt=pt, kt=kt, sub=sub, g=g: t.matmul(
                        pt[:, sub * 128:(sub + 1) * 128], mr2[:, kt * 128:kt * 128 + 128],
                        corr[:, CV + g * 128:CV + g * 128 + 128], start=False, stop=True),
                        inc=s_pe if sub == 3 else None) or vpe
                V_D = sd.emit("scalar", lambda s, pt=pt, kt4=kt4: s.copy(
                    arena[:, AVV + kt4 * TCW:AVV + (kt4 + 1) * TCW], pt[:, :]),
                    waits=[(s_pe, vpe)], inc=s_ac)
                bk.release(bi, s_ac, V_D)
            poolA.release(vi, s_pe, vpe)

            for hh in range(hpg):
                h = g * hpg + hh
                qi, qview, qrdy = load_slabA(wq, h * 128)
                qsum = emit_wsum(qview, qrdy, CQ + h * 128)
                Q_D, qpe = proj_fm(qview, qrdy, AQ, CQ + h * 128, QSC, [(s_ac, qsum)])
                poolA.release(qi, s_pe, qpe)
                QKV = [(s_ac, max(Q_D, K_D, V_D))]

                def attn_finish(qc, obi, opt, RIv, h=None):
                    bi2, pt2, w2 = bk.acquire()
                    v2 = sd.emit("tensor", lambda t, pt2=pt2: t.matmul(
                        pt2[:, 0:TCW], ones_lhsT(), arena[0:1, ARS:ARS + TCW],
                        start=True, stop=True), waits=w2 + [(s_ac, RIv)], inc=s_pe)
                    crow_busy[0] = (s_pe, v2)
                    RSB = sd.emit("scalar", lambda s, pt2=pt2: s.copy(
                        arena[:, ARS:ARS + TCW], pt2[:, 0:TCW]),
                        waits=[(s_pe, v2)], inc=s_ac)
                    bk.release(bi2, s_ac, RSB)
                    if kind == "a":
                        sd.emit("vector", lambda v, opt=opt: v.tensor_mul(
                            fscr[:, 0:TCW], opt[:, 0:TCW], arena[:, ARS:ARS + TCW]),
                            waits=[(s_ac, RSB)], inc=s_dv)
                        xv = sd.emit("vector", lambda v, h=h, qc=qc: v.tensor_add(
                            xb[:, h * T + qc * TCW:h * T + (qc + 1) * TCW],
                            xb[:, h * T + qc * TCW:h * T + (qc + 1) * TCW],
                            fscr[:, 0:TCW]), inc=s_dv)
                        xfin[h] = (s_dv, xv)
                        bk.release(obi, s_dv, xv)
                    else:
                        aslot = AEV if (h * NTC + qc) % 2 == 0 else 11264
                        av = sd.emit("vector", lambda v, opt=opt, aslot=aslot: v.tensor_mul(
                            arena[:, aslot:aslot + TCW], opt[:, 0:TCW],
                            arena[:, ARS:ARS + TCW]),
                            waits=[(s_ac, RSB), aev_busy[(h * NTC + qc) % 2]], inc=s_dv)
                        bk.release(obi, s_dv, av)
                        aev_busy[(h * NTC + qc) % 2] = sq.dma(
                            lambda e, h=h, qc=qc, aslot=aslot: e.dma_start(
                                dattn[qc * 128:(qc + 1) * 128, h * TCW:(h + 1) * TCW],
                                arena[:, aslot:aslot + TCW]),
                            waits=[(s_dv, av)])

                for qc in range(NTC):
                    nkt = 4 * qc + 4
                    obi, opt, ow = bk.acquire()
                    rbi, rpt, rw = bk.acquire()
                    pt_free = [None, None, None, None]
                    PV_D = None
                    pend = []

                    def do_rspv(kt, dep, pslot, nkt=nkt, opt=opt, rpt=rpt, ow=ow, rw=rw):
                        sd.emit("tensor", lambda t, opt=opt, pslot=pslot, kt=kt,
                                nkt=nkt: t.matmul(
                            opt[:, 0:TCW], arena[:, AVV + kt * 128:AVV + kt * 128 + 128],
                            arena[:, pslot:pslot + TCW],
                            start=(kt == 0), stop=(kt == nkt - 1)),
                            waits=[dep] + (ow if kt == 0 else []))
                        return sd.emit("tensor", lambda t, rpt=rpt, pslot=pslot, kt=kt,
                                       nkt=nkt: t.matmul(
                            rpt[0:1, 0:TCW], onescol, arena[:, pslot:pslot + TCW],
                            start=(kt == 0), stop=(kt == nkt - 1)),
                            waits=(rw if kt == 0 else []), inc=s_pe)

                    for kt in range(nkt):
                        sbi, spt, sw = bk.acquire()
                        stv = sd.emit("tensor", lambda t, spt=spt, kt=kt, qc=qc: t.matmul(
                            spt[:, 0:TCW], arena[:, AKK + kt * 128:AKK + kt * 128 + 128],
                            arena[:, AQ + qc * TCW:AQ + (qc + 1) * TCW],
                            start=True, stop=True),
                            waits=sw + (QKV if kt == 0 else []), inc=s_pe)
                        pslot = (APT, APT + TCW, 10240, 10752)[kt % 4]
                        ev = sd.emit("scalar", lambda s, spt=spt, pslot=pslot: s.activation(
                            arena[:, pslot:pslot + TCW], spt[:, 0:TCW], AF.Exp, bias=EXPB),
                            waits=[(s_pe, stv), pt_free[kt % 4]], inc=s_ac)
                        bk.release(sbi, s_ac, ev)
                        dep = (s_ac, ev)
                        if kt >= 4 * qc:
                            j = kt - 4 * qc
                            mv = sd.emit("vector", lambda v, pslot=pslot, j=j: v.tensor_mul(
                                arena[:, pslot:pslot + TCW], arena[:, pslot:pslot + TCW],
                                cst[:, CST_MASK + j * TCW:CST_MASK + (j + 1) * TCW]),
                                waits=[(s_ac, ev)], inc=s_dv)
                            dep = (s_dv, mv)
                        pend.append((kt, dep, pslot))
                        if kt == min(5, nkt - 1) and tail[0] is not None:
                            tail[0]()
                            tail[0] = None
                        if len(pend) > 3:
                            k0, d0, p0 = pend.pop(0)
                            PV_D = do_rspv(k0, d0, p0)
                            pt_free[k0 % 4] = (s_pe, PV_D)
                    while pend:
                        k0, d0, p0 = pend.pop(0)
                        PV_D = do_rspv(k0, d0, p0)
                        pt_free[k0 % 4] = (s_pe, PV_D)
                    RIF = sd.emit("vector", lambda v, rpt=rpt: v.reciprocal(
                        fscr[0:1, TCW:2 * TCW], rpt[0:1, 0:TCW]),
                        waits=[(s_pe, PV_D), crow_busy[0], ri_busy[0]], inc=s_dv)
                    bk.release(rbi, s_dv, RIF)
                    RI = sd.emit("scalar", lambda s: s.copy(
                        arena[0:1, ARS:ARS + TCW], fscr[0:1, TCW:2 * TCW]),
                        waits=[(s_dv, RIF)], inc=s_ac)
                    ri_busy[0] = (s_ac, RI)
                    tail[0] = (lambda qc=qc, obi=obi, opt=opt, RIv=RI, h=h:
                               attn_finish(qc, obi, opt, RIv, h))
        if tail[0] is not None:
            tail[0]()
            tail[0] = None
        corr_guard[0] = [(s_pe, s_pe.n)]

        if kind == "b":
            AOUT_W = sq.barrier()
            BO_RDY = gq.dma(gd(arena[0:1, ABO:ABO + D], dbo_b[0:1, :]))
            for tc in range(NTC):
                COL = sq.dma(lambda e, tc=tc: e.dma_start(
                    arena[:, AWO:AWO + NH * TCW], dattn[tc * 128:(tc + 1) * 128, :]),
                    waits=AOUT_W + [(s_pe, s_pe.n), (s_dv, s_dv.n)])
                for dt in range(NCH):
                    wi, wview, wrdy = load_slabA(dwo_b, dt * 128)
                    bi, pt, w = bk.acquire()
                    for c in range(NCH):
                        sd.emit("tensor", lambda t, pt=pt, wv=wview, c=c: t.matmul(
                            pt[:, 0:TCW], wv[:, c * 128:(c + 1) * 128],
                            arena[:, AWO + c * TCW:AWO + (c + 1) * TCW],
                            start=(c == 0), stop=False),
                            waits=(w + [wrdy, COL, BO_RDY]
                                   if c == 0 else []))
                    v = sd.emit("tensor", lambda t, pt=pt, dt=dt: t.matmul(
                        pt[:, 0:TCW], arena[0:1, ABO + dt * 128:ABO + dt * 128 + 128],
                        onesrow(), start=False, stop=True), inc=s_pe)
                    poolA.release(wi, s_pe, v)
                    xv = sd.emit("vector", lambda vv, pt=pt, dt=dt, tc=tc:
                                 vv.scalar_tensor_tensor(
                                     xb[:, dt * T + tc * TCW:dt * T + (tc + 1) * TCW],
                                     pt[:, 0:TCW], 1.0,
                                     xb[:, dt * T + tc * TCW:dt * T + (tc + 1) * TCW],
                                     ALU.mult, ALU.add),
                                 waits=[(s_pe, v)], inc=s_dv)
                    xfin[dt] = (s_dv, xv)
                    bk.release(bi, s_dv, xv)
            corr_guard[0] = [(s_pe, s_pe.n)]

    def emit_gating_units(XR_D, MR_D, box):
        base = [(s_dv, MR_D), (s_ac, GATE_RDY)] + CST_W

        def unit(tt):
            bi, pt, w = bk.acquire()
            for c in range(NCH):
                sd.emit("tensor", lambda t, pt=pt, c=c, tt=tt: t.matmul(
                    pt[:, 0:NE], xr[:, c * T + tt * 128:c * T + tt * 128 + 128],
                    gwb[:, c * NE:(c + 1) * NE], start=(c == 0), stop=False),
                    waits=[(s_dv, XR_D[c])] + ((w + base) if c == 0 else []))
            v = sd.emit("tensor", lambda t, pt=pt, tt=tt: t.matmul(
                pt[:, 0:NE], mr2[0:1, tt * 128:tt * 128 + 128], gsum[0:1, :],
                start=False, stop=True), inc=s_pe)
            nm = sd.emit("vector", lambda vv, pt=pt: vv.reduce_max(
                gsc[:, 84:85], pt[:, 0:NE], AX, negate=True), waits=[(s_pe, v)], inc=s_dv)
            ex = sd.emit("scalar", lambda s, pt=pt: s.activation(
                gsc[:, 0:NE], pt[:, 0:NE], AF.Exp, bias=gsc[:, 84:85]),
                waits=[(s_dv, nm)], inc=s_ac)
            bk.release(bi, s_ac, ex)
            for fn in (
                lambda vv: vv.reduce_sum(gsc[:, 82:83], gsc[:, 0:NE], AX),
                lambda vv: vv.reciprocal(gsc[:, 83:84], gsc[:, 82:83]),
                lambda vv: vv.tensor_scalar_mul(gsc[:, 0:NE], gsc[:, 0:NE], gsc[:, 83:84]),
                lambda vv: vv.tensor_add(gsc[:, 16:32], gsc[:, 0:NE], gbb[:, :]),
                lambda vv: vv.tensor_copy(gsc[:, 32:48], gsc[:, 16:32]),
            ):
                sd.emit("vector", fn, waits=[(s_ac, ex)], inc=s_dv)
            for _ in range(5):
                sd.emit("vector", lambda vv: vv.reduce_max(gsc[:, 80:81], gsc[:, 32:48], AX),
                        inc=s_dv)
                sd.emit("vector", lambda vv: vv.tensor_scalar(
                    gsc[:, 48:64], gsc[:, 32:48], gsc[:, 80:81], None, ALU.is_equal),
                    inc=s_dv)
                sd.emit("vector", lambda vv: vv.scalar_tensor_tensor(
                    gsc[:, 32:48], gsc[:, 48:64], -1e30, gsc[:, 32:48], ALU.mult, ALU.add),
                    inc=s_dv)
            sd.emit("vector", lambda vv: vv.reduce_max(gsc[:, 81:82], gsc[:, 32:48], AX),
                    inc=s_dv)
            sd.emit("vector", lambda vv: vv.tensor_scalar(
                gsc[:, 48:64], gsc[:, 16:32], gsc[:, 81:82], None, ALU.is_ge), inc=s_dv)
            CW = sd.emit("vector", lambda vv: vv.tensor_mul(
                gsc[:, 64:80], gsc[:, 0:NE], gsc[:, 48:64]), inc=s_dv)
            C16 = sd.emit("scalar", lambda s: s.copy(cwf[:, :], gsc[:, 64:80]),
                          waits=[(s_dv, CW)], inc=s_ac)
            po = (tt % 8) * 128
            trv = sd.emit("tensor", lambda t, po=po: t.transpose(
                ps16[0:NE, po:po + 128], cwf[:, :], cst[:, CST_IDEN:CST_IDEN + 128]),
                waits=[(s_ac, C16), cwtr_rel[tt % 8]], inc=s_pe)
            box[0] = sd.emit("scalar", lambda s, po=po, tt=tt: s.copy(
                cwT[:, tt * 128:(tt + 1) * 128], ps16[0:NE, po:po + 128]),
                waits=[(s_pe, trv)], inc=s_ac)
            cwtr_rel[tt % 8] = (s_ac, box[0])
        return [(lambda tt=tt: unit(tt)) for tt in range(NCH)]

    def emit_ffn(first, XR_D, MR_D):
        cwt_box = [None]
        gunits = emit_gating_units(XR_D, MR_D, cwt_box)
        base = [(s_dv, MR_D)] + CST_W
        for e in [NE] + list(range(NE)):
            shared = (e == NE)
            nht = NSHT if shared else NHT
            hw_ = SH if shared else FH
            CO1, CO3 = 0, hw_
            w1d, w3d, w2d = (dws1, dws3, dws2) if shared else (dw1, dw3, dw2)
            rbase = 0 if shared else e * NHT
            if first:
                d1, d3 = (dbs1, dbs3) if shared else (db1[e:e + 1, :], db3[e:e + 1, :])
                if shared:
                    CORR_W = [gq.dma(gd(corr[1:2, CO1:CO1 + hw_], dbs1[0:1, :]),
                                     waits=corr_guard[0]),
                              gq.dma(gd(corr[1:2, CO3:CO3 + hw_], dbs3[0:1, :]),
                                     waits=corr_guard[0])]
                else:
                    CORR_W = [gq.dma(gd(corr[1:2, CO1:CO1 + hw_], db1[e:e + 1, :]),
                                     waits=corr_guard[0]),
                              gq.dma(gd(corr[1:2, CO3:CO3 + hw_], db3[e:e + 1, :]),
                                     waits=corr_guard[0])]
            else:
                cwr = cache_wr.get(e, [])
                if shared:
                    CORR_W = [sq.dma(gd(corr[:, CO1:CO1 + hw_], dcs[0:2, :]),
                                     waits=corr_guard[0] + cwr),
                              sq.dma(gd(corr[:, CO3:CO3 + hw_], dcs[2:4, :]),
                                     waits=corr_guard[0] + cwr)]
                else:
                    CORR_W = [sq.dma(gd(corr[:, CO1:CO1 + hw_], dcw1[2 * e:2 * e + 2, :]),
                                     waits=corr_guard[0] + cwr),
                              sq.dma(gd(corr[:, CO3:CO3 + hw_], dcw3[2 * e:2 * e + 2, :]),
                                     waits=corr_guard[0] + cwr)]
            C_RDY = None
            BS2_RDY = None
            if not shared:
                while gunits:
                    gunits.pop(0)()
                if e in Cpend:
                    CR = Cpend.pop(e)
                else:
                    cwaits = [(s_ac, cwt_box[0]), crow_busy[0]]
                    if C_cons[0]:
                        cwaits.append((s_dv, C_cons[0]))
                    CR = sq.dma(lambda s2, e=e: s2.dma_start(
                        arena[0:1, CB:CB + T], cwT[e:e + 1, :]), waits=cwaits)
                bi, pt, w = bk.acquire()
                lastpe = None
                C_RDY = None
                for tc in range(NTC):
                    lastpe = sd.emit("tensor", lambda t, pt=pt, tc=tc: t.matmul(
                        pt[:, 0:TCW], ones_lhsT(),
                        arena[0:1, CB + tc * TCW:CB + (tc + 1) * TCW],
                        start=True, stop=True),
                        waits=(w + [CR]) if tc == 0 else [(s_ac, C_RDY)], inc=s_pe)
                    C_RDY = sd.emit("scalar", lambda s, pt=pt, tc=tc: s.copy(
                        arena[:, CB + tc * TCW:CB + (tc + 1) * TCW], pt[:, 0:TCW]),
                        waits=[(s_pe, lastpe)], inc=s_ac)
                crow_busy[0] = (s_pe, lastpe)
                bk.release(bi, s_ac, C_RDY)
            else:
                cwaits = [crow_busy[0]]
                if C_cons[0]:
                    cwaits.append((s_dv, C_cons[0]))
                BS2_RDY = gq.dma(gd(arena[0:1, CB:CB + T], dbs2[0:1, :]), waits=cwaits)
            groups = ([list(range(0, 6)), list(range(6, 11))] if not shared else
                      [list(range(0, 6)), list(range(6, 12)), list(range(12, 18)),
                       list(range(18, 22))])
            for gi, grp in enumerate(groups):
                g0 = grp[0]
                H_D = None
                a1pe = a3pe = None
                for ht in grp:
                    if shared and gunits:
                        gunits.pop(0)()
                        if gunits:
                            gunits.pop(0)()
                    i1, v1, r1 = load_slabA(w1d, (rbase + ht) * 128)
                    i3, v3, r3 = load_slabA(w3d, (rbase + ht) * 128)
                    if first:
                        e1 = emit_wsum(v1, r1, CO1 + ht * 128)
                        e3 = emit_wsum(v3, r3, CO3 + ht * 128)
                        cwx = [(s_ac, max(e1, e3))] + CORR_W
                    else:
                        cwx = CORR_W
                    for tc in range(NTC):
                        b1i, p1, w1w = bk.acquire()
                        b3i, p3, w3w = bk.acquire()
                        for c in range(NCH):
                            sd.emit("tensor", lambda t, p1=p1, v1=v1, c=c, tc=tc: t.matmul(
                                p1[:, 0:TCW], v1[:, c * 128:(c + 1) * 128],
                                xr[:, c * T + tc * TCW:c * T + (tc + 1) * TCW],
                                start=(c == 0), stop=False),
                                waits=[(s_dv, XR_D[c])] + ((w1w + [r1] + cwx + base)
                                                          if c == 0 else []))
                        a1pe = sd.emit("tensor", lambda t, p1=p1, CO1=CO1, ht=ht, tc=tc:
                                       t.matmul(
                            p1[:, 0:TCW], corr[:, CO1 + ht * 128:CO1 + ht * 128 + 128],
                            mr2[:, tc * TCW:(tc + 1) * TCW], start=False, stop=True),
                            inc=s_pe)
                        for c in range(NCH):
                            sd.emit("tensor", lambda t, p3=p3, v3=v3, c=c, tc=tc: t.matmul(
                                p3[:, 0:TCW], v3[:, c * 128:(c + 1) * 128],
                                xr[:, c * T + tc * TCW:c * T + (tc + 1) * TCW],
                                start=(c == 0), stop=False),
                                waits=[(s_dv, XR_D[c])] + ((w3w + [r3] + cwx)
                                                          if c == 0 else []))
                        a3pe = sd.emit("tensor", lambda t, p3=p3, CO3=CO3, ht=ht, tc=tc:
                                       t.matmul(
                            p3[:, 0:TCW], corr[:, CO3 + ht * 128:CO3 + ht * 128 + 128],
                            mr2[:, tc * TCW:(tc + 1) * TCW], start=False, stop=True),
                            inc=s_pe)
                        sfx = (tc % 2) * TCW
                        sv = sd.emit("scalar", lambda s, p1=p1, sfx=sfx: s.activation(
                            fscr[:, sfx:sfx + TCW], p1[:, 0:TCW], AF.Silu),
                            waits=[(s_pe, a1pe), s1_busy[tc % 2]], inc=s_ac)
                        bk.release(b1i, s_ac, sv)
                        hcol = HBC + (ht - g0) * T + tc * TCW
                        if shared:
                            hv = sd.emit("vector", lambda vv, p3=p3, sfx=sfx, hcol=hcol:
                                         vv.scalar_tensor_tensor(
                                             arena[:, hcol:hcol + TCW], p3[:, 0:TCW], 1.0,
                                             fscr[:, sfx:sfx + TCW], ALU.mult, ALU.mult),
                                         waits=[(s_pe, a3pe), (s_ac, sv)], inc=s_dv)
                        else:
                            sd.emit("vector", lambda vv, p3=p3, sfx=sfx:
                                    vv.scalar_tensor_tensor(
                                        fscr[:, sfx:sfx + TCW], p3[:, 0:TCW], 1.0,
                                        fscr[:, sfx:sfx + TCW], ALU.mult, ALU.mult),
                                    waits=[(s_pe, a3pe), (s_ac, sv), (s_ac, C_RDY)],
                                    inc=s_dv)
                            hv = sd.emit("vector", lambda vv, sfx=sfx, hcol=hcol, tc=tc:
                                         vv.tensor_mul(
                                             arena[:, hcol:hcol + TCW],
                                             fscr[:, sfx:sfx + TCW],
                                             arena[:, CB + tc * TCW:CB + (tc + 1) * TCW]),
                                         inc=s_dv)
                        s1_busy[tc % 2] = (s_dv, hv)
                        bk.release(b3i, s_dv, hv)
                        H_D = hv
                    poolA.release(i1, s_pe, a1pe)
                    poolA.release(i3, s_pe, a3pe)
                if first and gi == len(groups) - 1:
                    if shared:
                        cache_wr[e] = [
                            sq.dma(gd(dcs[0:2, :], corr[:, CO1:CO1 + hw_]),
                                   waits=[(s_ac, s_ac.n)]),
                            sq.dma(gd(dcs[2:4, :], corr[:, CO3:CO3 + hw_]),
                                   waits=[(s_ac, s_ac.n)])]
                    else:
                        cache_wr[e] = [
                            sq.dma(gd(dcw1[2 * e:2 * e + 2, :], corr[:, CO1:CO1 + hw_]),
                                   waits=[(s_ac, s_ac.n)]),
                            sq.dma(gd(dcw3[2 * e:2 * e + 2, :], corr[:, CO3:CO3 + hw_]),
                                   waits=[(s_ac, s_ac.n)])]
                nxt = 0 if shared else e + 1
                if gi == len(groups) - 1 and nxt < NE:
                    Cpend[nxt] = sq.dma(
                        lambda s2, e1=nxt: s2.dma_start(
                            arena[0:1, CB:CB + T], cwT[e1:e1 + 1, :]),
                        waits=[(s_ac, cwt_box[0]), crow_busy[0], (s_dv, s_dv.n)])
                for dq in range(8):
                    slabs = []
                    for hc in grp:
                        slabs.append(load_slabB(w2d, ((rbase + hc) * 8 + dq) * 128))
                    lastpe = None
                    for j in range(2):
                        dt = dq * 2 + j
                        for tc in range(NTC):
                            bi, pt, w = bk.acquire()
                            nacc = len(grp) + (1 if (e == 0 and gi == 0) else 0) + \
                                (1 if (shared and gi == 0) else 0)
                            k = 0
                            for hi, (ib, vb, rb) in enumerate(slabs):
                                hcol = HBC + hi * T + tc * TCW
                                sd.emit("tensor",
                                        lambda t, pt=pt, vb=vb, j=j, hcol=hcol,
                                        st=(k == 0), sp=(k == nacc - 1): t.matmul(
                                            pt[:, 0:TCW], vb[:, j * 128:(j + 1) * 128],
                                            arena[:, hcol:hcol + TCW], start=st, stop=sp),
                                        waits=(w + [rb, (s_dv, H_D)]
                                               if k == 0 else [rb]),
                                        inc=s_pe if k == nacc - 1 else None)
                                k += 1
                            if e == 0 and gi == 0:
                                sd.emit("tensor", lambda t, pt=pt, dt=dt, tc=tc,
                                        sp=(k == nacc - 1): t.matmul(
                                            pt[:, 0:TCW], b2b[:, dt * 128:(dt + 1) * 128],
                                            cwT[:, tc * TCW:(tc + 1) * TCW],
                                            start=False, stop=sp),
                                        waits=[(s_ac, cwt_box[0])],
                                        inc=s_pe if k == nacc - 1 else None)
                                k += 1
                            if shared and gi == 0:
                                sd.emit("tensor", lambda t, pt=pt, dt=dt,
                                        sp=(k == nacc - 1): t.matmul(
                                            pt[:, 0:TCW],
                                            arena[0:1, CB + dt * 128:CB + dt * 128 + 128],
                                            onesrow(), start=False, stop=sp),
                                        waits=[BS2_RDY],
                                        inc=s_pe if k == nacc - 1 else None)
                                k += 1
                            lastpe = s_pe.n
                            xv = sd.emit("vector", lambda vv, pt=pt, dt=dt, tc=tc:
                                         vv.scalar_tensor_tensor(
                                             xb[:, dt * T + tc * TCW:dt * T + (tc + 1) * TCW],
                                             pt[:, 0:TCW], 1.0,
                                             xb[:, dt * T + tc * TCW:dt * T + (tc + 1) * TCW],
                                             ALU.mult, ALU.add),
                                         waits=[(s_pe, lastpe)], inc=s_dv)
                            xfin[dt] = (s_dv, xv)
                            bk.release(bi, s_dv, xv)
                    for (ib, vb, rb) in slabs:
                        poolB.release(ib, s_pe, lastpe)
                if shared and gi == 0:
                    crow_busy[0] = (s_pe, s_pe.n)
                C_cons[0] = s_dv.n
            corr_guard[0] = [(s_pe, s_pe.n)] + cache_wr.get(e, [])

    kinds = ["attn_a", "ffn1", "attn_b", "ffn2"]
    for si in range(n_sub):
        XR_D, MR_D = emit_ln()
        k = kinds[si]
        if k == "attn_a":
            emit_attn("a", XR_D, MR_D)
        elif k == "attn_b":
            emit_attn("b", XR_D, MR_D)
        else:
            emit_ffn(k == "ffn1", XR_D, MR_D)

    fin = [(s_pe, s_pe.n), (s_dv, s_dv.n), (s_ac, s_ac.n)]
    st_busy = [None, None]
    for c in range(NCH):
        for piece in range(4):
            sl = (piece % 2) * TCW
            av = sd.emit("scalar", lambda s, c=c, sl=sl, piece=piece: s.copy(
                fscr[:, sl:sl + TCW],
                xb[:, c * T + piece * TCW:c * T + (piece + 1) * TCW]),
                waits=fin + [st_busy[piece % 2]], inc=s_ac)
            st_busy[piece % 2] = sq.dma(
                lambda e, c=c, sl=sl, piece=piece: e.dma_start(
                    dout[c * 128:(c + 1) * 128, piece * TCW:(piece + 1) * TCW],
                    fscr[:, sl:sl + TCW]),
                waits=[(s_ac, av)])
    fin_waits = sq.barrier() + gq.barrier()

    def final_wait(e, fw=tuple(fin_waits)):
        ins = None
        for sem, val in fw:
            ins = e.wait_ge(sem.h, val)
        return ins
    sd.emit("sync", final_wait)

    with nc.Block() as block:
        sd.drain(block)
    return nc


def _swz_proj(w):
    return np.ascontiguousarray(
        w.reshape(16, 128, 16, 128).transpose(2, 1, 0, 3).reshape(2048, 2048))


def _host_arrays(inputs):
    f = np.float32

    def c(x):
        return np.ascontiguousarray(x).astype(f)

    a = {}
    wq = np.asarray(inputs["L0_attn_a_Wq"])
    a["wq_a"] = c(wq.reshape(16, 16, 128, 128).transpose(0, 2, 1, 3).reshape(2048, 2048))
    a["wk_a"] = c(np.asarray(inputs["L0_attn_a_Wk"]).reshape(4, 16, 128, 128)
                  .transpose(0, 2, 1, 3).reshape(512, 2048))
    a["wv_a"] = c(np.asarray(inputs["L0_attn_a_Wv"]).reshape(4, 16, 128, 128)
                  .transpose(0, 2, 1, 3).reshape(512, 2048))
    a["bq_a"] = c(np.asarray(inputs["L0_attn_a_bq"]).reshape(1, -1))
    a["bk_a"] = c(np.asarray(inputs["L0_attn_a_bk"]).reshape(1, -1))
    a["bv_a"] = c(np.asarray(inputs["L0_attn_a_bv"]).reshape(1, -1))
    a["wq_b"] = c(_swz_proj(np.asarray(inputs["L0_attn_b_w_q"])))
    a["wk_b"] = c(_swz_proj(np.asarray(inputs["L0_attn_b_w_k"])))
    a["wv_b"] = c(_swz_proj(np.asarray(inputs["L0_attn_b_w_v"])))
    a["wo_b"] = c(_swz_proj(np.asarray(inputs["L0_attn_b_w_o"])))
    a["bq_b"] = c(np.asarray(inputs["L0_attn_b_b_q"]).reshape(1, -1))
    a["bk_b"] = c(np.asarray(inputs["L0_attn_b_b_k"]).reshape(1, -1))
    a["bv_b"] = c(np.asarray(inputs["L0_attn_b_b_v"]).reshape(1, -1))
    a["bo_b"] = c(np.asarray(inputs["L0_attn_b_b_o"]).reshape(1, -1))
    a["gw"] = c(np.asarray(inputs["L0_ffn_gate_w"]).T.reshape(16, 128, 16)
                .transpose(1, 0, 2).reshape(128, 256))
    a["gb"] = c(np.asarray(inputs["L0_ffn_gate_b"]).reshape(1, -1))
    a["w1"] = c(np.asarray(inputs["L0_ffn_w1"]).transpose(0, 2, 1)
                .reshape(16, 16, 128, 11, 128).transpose(0, 3, 2, 1, 4)
                .reshape(16 * 11 * 128, 2048))
    a["w3"] = c(np.asarray(inputs["L0_ffn_w3"]).transpose(0, 2, 1)
                .reshape(16, 16, 128, 11, 128).transpose(0, 3, 2, 1, 4)
                .reshape(16 * 11 * 128, 2048))
    a["w2"] = c(np.asarray(inputs["L0_ffn_w2"]).transpose(0, 2, 1)
                .reshape(16, 11, 128, 8, 2, 128).transpose(0, 1, 3, 2, 4, 5)
                .reshape(16 * 11 * 8 * 128, 256))
    a["b1"] = c(np.asarray(inputs["L0_ffn_b1"]))
    a["b3"] = c(np.asarray(inputs["L0_ffn_b3"]))
    a["b2"] = c(np.asarray(inputs["L0_ffn_b2"]))
    a["ws1"] = c(np.asarray(inputs["L0_ffn_ws1"]).T.reshape(16, 128, 22, 128)
                 .transpose(2, 1, 0, 3).reshape(22 * 128, 2048))
    a["ws3"] = c(np.asarray(inputs["L0_ffn_ws3"]).T.reshape(16, 128, 22, 128)
                 .transpose(2, 1, 0, 3).reshape(22 * 128, 2048))
    a["ws2"] = c(np.asarray(inputs["L0_ffn_ws2"]).T.reshape(22, 128, 8, 2, 128)
                 .transpose(0, 2, 1, 3, 4).reshape(22 * 8 * 128, 256))
    a["bs1"] = c(np.asarray(inputs["L0_ffn_bs1"]).reshape(1, -1))
    a["bs3"] = c(np.asarray(inputs["L0_ffn_bs3"]).reshape(1, -1))
    a["bs2"] = c(np.asarray(inputs["L0_ffn_bs2"]).reshape(1, -1))
    cstv = np.zeros((128, 2176), dtype=np.float16)
    p = np.arange(128)[:, None]
    fc = np.arange(512)[None, :]
    for j in range(4):
        cstv[:, j * 512:(j + 1) * 512] = (fc >= (p + 128 * j)).astype(np.float16)
    cstv[:, 2048:2176] = np.eye(128, dtype=np.float16)
    a["cst"] = cstv
    a["cfl"] = np.tile(np.array([[1e-5, -4.0]], dtype=np.float32), (128, 1))
    return a


def _swz_x(xbat):
    return np.ascontiguousarray(
        xbat.T.reshape(16, 128, 2048).transpose(1, 0, 2).reshape(128, 16 * 2048)
    ).astype(np.float32)


def kernel(**inputs):
    shared = _host_arrays(inputs)
    x = np.asarray(inputs["L0_attn_a_x"])
    nc = build_program(4)
    in_maps = []
    for b in range(8):
        m = dict(shared)
        m["xin"] = _swz_x(np.asarray(x[b], dtype=np.float32))
        in_maps.append(m)
    res = run_bass_kernel_spmd(nc, in_maps, core_ids=list(range(8)))
    out = np.empty((8, 2048, 2048), dtype=np.float32)
    for b in range(8):
        out[b] = np.asarray(res.results[b]["out"]).reshape(2048, 2048).T
    return out


# version 07 (final)
# speedup vs baseline: 1.2378x; 1.2378x over previous
import numpy as np

import concourse.bass as bass
import concourse.mybir as mybir
from concourse.bass_utils import run_bass_kernel_spmd

F32 = mybir.dt.float32
F16 = mybir.dt.float16
AF = mybir.ActivationFunctionType
ALU = mybir.AluOpType
AX = mybir.AxisListType.X

D = 2048
T = 2048
NCH = 16
TCW = 512
NTC = 4
NH = 16
HD = 128
FH = 1408
NHT = 11
SH = 2816
NSHT = 22
NE = 16
INV_D = 1.0 / 2048.0
QSC = float(1.0 / np.sqrt(128.0))
EXPB = -4.0

ACOL = 14336
HBC = 0
CB = 12288
AQ = 0
AKK = 2048
AVV = 4096
APT = 6144
ARS = 7168
AEV = 7680
AWO = 0
ABO = 8192
LRB = 0
LSQ0, LSQ1 = 2048, 4096
LM, LV, LR = 6144, 8192, 10240
CST_MASK = 0
CST_IDEN = 2048
CST_ONEC = 2047


class Sem:
    def __init__(self, nc, name):
        self.h = nc.alloc_semaphore(name)
        self.n = 0


class Sched:
    ENGS = ("tensor", "vector", "scalar", "gpsimd", "sync")

    def __init__(self):
        self.q = {e: [] for e in self.ENGS}
        self.primary = {}

    def emit(self, eng, fn, waits=(), inc=None, inc16=None):
        ws = [w for w in waits if w is not None]
        psem = self.primary.get(eng)
        if psem is not None:
            if psem.n:
                ws.append((psem, psem.n))
            if inc is None:
                inc = psem
            assert inc is psem, "vector/scalar ops must inc their primary sem"
        incs = []
        ret = None
        if inc is not None:
            inc.n += 1
            ret = inc.n
            incs.append((inc, 1))
        if inc16 is not None:
            inc16.n += 16
            ret = inc16.n
            incs.append((inc16, 16))
        self.q[eng].append((ws, fn, incs))
        return ret

    def drain(self, block):
        def mk(name):
            def body(e):
                seen = {}
                for waits, fn, incs in self.q[name]:
                    ws = []
                    for sem, val in waits:
                        if val > seen.get(id(sem), 0):
                            ws.append((sem, val))
                            seen[id(sem)] = val
                    for sem, val in ws[:-1]:
                        e.wait_ge(sem.h, val)
                    ins = fn(e)
                    if ws:
                        ins._wait_ge(ws[-1][0].h, ws[-1][1])
                    for sem, k in incs:
                        ins = ins.then_inc(sem.h, k)
            return body
        block.tensor(mk("tensor"))
        block.vector(mk("vector"))
        block.scalar(mk("scalar"))
        block.gpsimd(mk("gpsimd"))
        block.sync(mk("sync"))


class Banks:
    def __init__(self, tiles):
        self.t = tiles
        self.rel = [None] * len(tiles)
        self.busy = [False] * len(tiles)
        self.rr = 0

    def acquire(self):
        for k in range(len(self.t)):
            i = (self.rr + k) % len(self.t)
            if not self.busy[i]:
                self.rr = (i + 1) % len(self.t)
                self.busy[i] = True
                w = self.rel[i]
                return i, self.t[i], ([w] if w else [])
        raise RuntimeError("psum banks exhausted")

    def release(self, i, sem, val):
        self.busy[i] = False
        self.rel[i] = (sem, val)


class Slots:
    def __init__(self, ten, n, width):
        self.ten = ten
        self.n = n
        self.w = width
        self.rel = [None] * n
        self.rr = 0

    def acquire(self):
        i = self.rr
        self.rr = (i + 1) % self.n
        w = self.rel[i]
        return i, self.ten[:, i * self.w:(i + 1) * self.w], ([w] if w else [])

    def release(self, i, sem, val):
        self.rel[i] = (sem, val)


class DmaQ:

    def __init__(self, nc, sd, eng, name, nlanes):
        self.sd = sd
        self.eng = eng
        self.lanes = [Sem(nc, f"{name}{i}") for i in range(nlanes)]
        self.i = 0

    def dma(self, fn, waits=()):
        lane = self.lanes[self.i % len(self.lanes)]
        self.i += 1
        w = [x for x in waits if x is not None]
        if lane.n:
            w.append((lane, lane.n))
        rdy = self.sd.emit(self.eng, fn, waits=w, inc16=lane)
        return (lane, rdy)

    def barrier(self):
        return [(l, l.n) for l in self.lanes if l.n]


def build_program(n_sub=4):
    nc = bass.Bass("TRN2", target_bir_lowering=False)

    dxin = nc.dram_tensor("xin", [128, NCH * T], F32, kind="ExternalInput")
    dout = nc.dram_tensor("out", [NCH * 128, T], F32, kind="ExternalOutput")
    dwq_a = nc.dram_tensor("wq_a", [NH * 128, D], F32, kind="ExternalInput")
    dwk_a = nc.dram_tensor("wk_a", [4 * 128, D], F32, kind="ExternalInput")
    dwv_a = nc.dram_tensor("wv_a", [4 * 128, D], F32, kind="ExternalInput")
    dbq_a = nc.dram_tensor("bq_a", [1, NH * HD], F32, kind="ExternalInput")
    dbk_a = nc.dram_tensor("bk_a", [1, 4 * HD], F32, kind="ExternalInput")
    dbv_a = nc.dram_tensor("bv_a", [1, 4 * HD], F32, kind="ExternalInput")
    dwq_b = nc.dram_tensor("wq_b", [NH * 128, D], F32, kind="ExternalInput")
    dwk_b = nc.dram_tensor("wk_b", [NH * 128, D], F32, kind="ExternalInput")
    dwv_b = nc.dram_tensor("wv_b", [NH * 128, D], F32, kind="ExternalInput")
    dwo_b = nc.dram_tensor("wo_b", [NH * 128, D], F32, kind="ExternalInput")
    dbq_b = nc.dram_tensor("bq_b", [1, D], F32, kind="ExternalInput")
    dbk_b = nc.dram_tensor("bk_b", [1, D], F32, kind="ExternalInput")
    dbv_b = nc.dram_tensor("bv_b", [1, D], F32, kind="ExternalInput")
    dbo_b = nc.dram_tensor("bo_b", [1, D], F32, kind="ExternalInput")
    dgw = nc.dram_tensor("gw", [128, NCH * NE], F32, kind="ExternalInput")
    dgb = nc.dram_tensor("gb", [1, NE], F32, kind="ExternalInput")
    dw1 = nc.dram_tensor("w1", [NE * NHT * 128, D], F32, kind="ExternalInput")
    dw3 = nc.dram_tensor("w3", [NE * NHT * 128, D], F32, kind="ExternalInput")
    dw2 = nc.dram_tensor("w2", [NE * NHT * 8 * 128, 256], F32, kind="ExternalInput")
    db1 = nc.dram_tensor("b1", [NE, FH], F32, kind="ExternalInput")
    db3 = nc.dram_tensor("b3", [NE, FH], F32, kind="ExternalInput")
    db2 = nc.dram_tensor("b2", [NE, D], F32, kind="ExternalInput")
    dws1 = nc.dram_tensor("ws1", [NSHT * 128, D], F32, kind="ExternalInput")
    dws3 = nc.dram_tensor("ws3", [NSHT * 128, D], F32, kind="ExternalInput")
    dws2 = nc.dram_tensor("ws2", [NSHT * 8 * 128, 256], F32, kind="ExternalInput")
    dbs1 = nc.dram_tensor("bs1", [1, SH], F32, kind="ExternalInput")
    dbs3 = nc.dram_tensor("bs3", [1, SH], F32, kind="ExternalInput")
    dbs2 = nc.dram_tensor("bs2", [1, D], F32, kind="ExternalInput")
    dcst = nc.dram_tensor("cst", [128, 2176], F16, kind="ExternalInput")
    dcfl = nc.dram_tensor("cfl", [128, 2], F32, kind="ExternalInput")
    dbc1 = nc.dram_tensor("bc1", [128, NE * NHT], F32, kind="ExternalInput")
    dbc3 = nc.dram_tensor("bc3", [128, NE * NHT], F32, kind="ExternalInput")
    dbcs = nc.dram_tensor("bcs", [128, 2 * NSHT + 16], F32, kind="ExternalInput")
    dbca = nc.dram_tensor("bca", [128, 20], F32, kind="ExternalInput")
    dbcb = nc.dram_tensor("bcb", [128, 48], F32, kind="ExternalInput")

    dattn = nc.dram_tensor("attnout", [NTC * 128, NH * TCW], F16, kind="Internal")

    xb = nc.alloc_sbuf_tensor("xb", [128, NCH * T], F16)
    xr = nc.alloc_sbuf_tensor("xr", [128, NCH * T], F16)
    poolA_t = nc.alloc_sbuf_tensor("poolA", [128, 3 * 2048], F16)
    poolB_t = nc.alloc_sbuf_tensor("poolB", [128, 12 * 256], F16)
    arena = nc.alloc_sbuf_tensor("arena", [128, ACOL], F16)
    fscr = nc.alloc_sbuf_tensor("fscr", [128, 1024], F32)
    gsc = nc.alloc_sbuf_tensor("gsc", [128, 88], F32)
    cst = nc.alloc_sbuf_tensor("cstb", [128, 2176], F16)
    bc1 = nc.alloc_sbuf_tensor("bc1_s", [128, NE * NHT], F32)
    bc3 = nc.alloc_sbuf_tensor("bc3_s", [128, NE * NHT], F32)
    bcs = nc.alloc_sbuf_tensor("bcs_s", [128, 2 * NSHT + 16], F32)
    bca = nc.alloc_sbuf_tensor("bca_s", [128, 20], F32)
    bcaq = nc.alloc_sbuf_tensor("bcaq", [128, 16], F32)
    bcb = nc.alloc_sbuf_tensor("bcb_s", [128, 48], F32)
    bcbq = nc.alloc_sbuf_tensor("bcbq", [128, 16], F32)
    bvra = nc.alloc_sbuf_tensor("bvra", [1, 4 * HD], F16)
    psum2 = nc.alloc_sbuf_tensor("ptps", [128, 2 * TCW], F16)
    bvrb = nc.alloc_sbuf_tensor("bvrb", [1, D], F16)
    gwb = nc.alloc_sbuf_tensor("gwb", [128, NCH * NE], F16)
    gbr = nc.alloc_sbuf_tensor("gbr", [1, NE], F16)
    gbb = nc.alloc_sbuf_tensor("gbb", [128, NE], F16)
    b2b = nc.alloc_sbuf_tensor("b2b", [NE, D], F16)
    cwT = nc.alloc_sbuf_tensor("cwT", [NE, T], F16)
    cwf = nc.alloc_sbuf_tensor("cwf", [128, NE], F16)
    cfl = nc.alloc_sbuf_tensor("cflb", [128, 2], F32)
    cflh = nc.alloc_sbuf_tensor("cflhb", [128, 2], F16)
    nc.const_aps.register(F32, 1e-5, cfl[:, 0:1])
    nc.const_aps.register(F32, EXPB, cfl[:, 1:2])
    nc.const_aps.register(F16, 1e-5, cflh[:, 0:1])
    nc.const_aps.register(F16, EXPB, cflh[:, 1:2])
    assert nc.sbuf_bytes_remaining >= 0, f"SBUF over by {-nc.sbuf_bytes_remaining} B"

    psf = [nc.alloc_psum_tensor(f"ps{i}", [128, 512], F32) for i in range(7)]
    ps16 = nc.alloc_psum_tensor("ps16", [128, 1024], F16)
    bk = Banks(psf)
    poolA = Slots(poolA_t, 3, 2048)
    poolB = Slots(poolB_t, 12, 256)

    sd = Sched()
    s_pe = Sem(nc, "s_pe")
    s_dv = Sem(nc, "s_dv")
    s_ac = Sem(nc, "s_ac")
    sd.primary["vector"] = s_dv
    sd.primary["scalar"] = s_ac
    gq = DmaQ(nc, sd, "gpsimd", "gl", 8)
    sq = DmaQ(nc, sd, "sync", "sl", 6)

    onescol = cst[:, CST_ONEC:CST_ONEC + 1]

    def ones_lhsT(m=128):
        return cst[0:1, 0:m]

    def onesrow(n=TCW):
        return cst[0:1, 0:n]

    def gd(dst, src):
        return lambda e, dst=dst, src=src: e.dma_start(dst, src)

    crow_busy = [None]
    ri_busy = [None]
    aev_busy = [None, None]
    s1_busy = [None, None]
    C_cons = [None]
    cwtr_rel = [None] * 8
    Cpend = {}
    xfin = [None] * NCH

    X_RDY = gq.dma(gd(xb[:, :], dxin[:, :]))
    CST_W = [gq.dma(gd(cst[:, :], dcst[:, :])),
             gq.dma(gd(gwb[:, :], dgw[:, :])),
             gq.dma(gd(gbr[:, :], dgb[:, :])),
             gq.dma(gd(cfl[:, :], dcfl[:, :])),
             gq.dma(gd(cflh[:, :], dcfl[:, :])),
             gq.dma(gd(b2b[:, :], db2[:, :]))]
    CST_W += [gq.dma(gd(bc1[:, :], dbc1[:, :])),
              gq.dma(gd(bc3[:, :], dbc3[:, :])),
              gq.dma(gd(bcs[:, :], dbcs[:, :])),
              gq.dma(gd(bca[:, :], dbca[:, :])),
              gq.dma(gd(bcb[:, :], dbcb[:, :])),
              gq.dma(gd(bvra[:, :], dbv_a[:, :])),
              gq.dma(gd(bvrb[:, :], dbv_b[:, :]))]
    BQS = sd.emit("scalar", lambda s: s.mul(bcaq[:, :], bca[:, 0:16], QSC),
                  waits=CST_W, inc=s_ac)
    BQS = sd.emit("scalar", lambda s: s.mul(bcbq[:, :], bcb[:, 0:16], QSC),
                  waits=CST_W, inc=s_ac)
    CST_W = CST_W + [(s_ac, BQS)]

    bi, pt, w = bk.acquire()
    vv = sd.emit("tensor", lambda t, pt=pt: t.matmul(pt[:, 0:NE], ones_lhsT(), gbr[0:1, :],
                                                     start=True, stop=True),
                 waits=w + CST_W, inc=s_pe)
    GB = sd.emit("scalar", lambda s, pt=pt: s.copy(gbb[:, :], pt[:, 0:NE]),
                 waits=[(s_pe, vv)], inc=s_ac)
    bk.release(bi, s_ac, GB)
    GATE_RDY = GB

    def emit_ln():
        bw = [(s_pe, s_pe.n), (s_dv, s_dv.n), (s_ac, s_ac.n), X_RDY] + CST_W
        sbk = [bk.acquire() for _ in range(4)]
        SUM_D = None
        for c in range(NCH):
            for tc in range(NTC):
                bi, pt, w = sbk[tc]
                SUM_D = sd.emit("tensor",
                                lambda t, pt=pt, c=c, tc=tc: t.matmul(
                                    pt[0:1, 0:TCW], onescol,
                                    xb[:, c * T + tc * TCW:c * T + (tc + 1) * TCW],
                                    start=(c == 0), stop=(c == NCH - 1)),
                                waits=[xfin[c], X_RDY] + (w + CST_W
                                                          if c == 0 else []),
                                inc=s_pe if c == NCH - 1 else None) or SUM_D
        M_D = None
        for tc in range(NTC):
            bi, pt, _ = sbk[tc]
            M_D = sd.emit("scalar", lambda s, pt=pt, tc=tc: s.mul(
                arena[0:1, LM + tc * TCW:LM + (tc + 1) * TCW], pt[0:1, 0:TCW], INV_D),
                waits=[(s_pe, SUM_D)], inc=s_ac)
            bk.release(bi, s_ac, M_D)
        qbk = [bk.acquire() for _ in range(4)]
        sqrel = [None, None]
        SQ_D = None
        for c in range(NCH):
            slot = LSQ0 if c % 2 == 0 else LSQ1
            SQ = sd.emit("scalar", lambda s, c=c, slot=slot: s.activation(
                arena[:, slot:slot + T], xb[:, c * T:(c + 1) * T], AF.Square),
                waits=bw + [xfin[c]] + ([sqrel[c % 2]] if sqrel[c % 2] else []),
                inc=s_ac)
            for tc in range(NTC):
                bi, pt, w = qbk[tc]
                SQ_D = sd.emit("tensor", lambda t, pt=pt, slot=slot, tc=tc, c=c: t.matmul(
                    pt[0:1, 0:TCW], onescol,
                    arena[:, slot + tc * TCW:slot + (tc + 1) * TCW],
                    start=(c == 0), stop=(c == NCH - 1)),
                    waits=[(s_ac, SQ)] + (w if c == 0 else []),
                    inc=s_pe if tc == NTC - 1 else None) or SQ_D
            sqrel[c % 2] = (s_pe, s_pe.n)
        MSQ = sd.emit("vector", lambda v: v.tensor_mul(
            arena[0:1, LV:LV + T], arena[0:1, LM:LM + T], arena[0:1, LM:LM + T]),
            waits=[(s_ac, M_D)], inc=s_dv)
        VAR_D = None
        for tc in range(NTC):
            bi, pt, _ = qbk[tc]
            VAR_D = sd.emit("vector", lambda v, pt=pt, tc=tc: v.scalar_tensor_tensor(
                arena[0:1, LV + tc * TCW:LV + (tc + 1) * TCW], pt[0:1, 0:TCW], INV_D,
                arena[0:1, LV + tc * TCW:LV + (tc + 1) * TCW], ALU.mult, ALU.subtract),
                waits=[(s_pe, SQ_D), (s_dv, MSQ)], inc=s_dv)
            bk.release(bi, s_dv, VAR_D)
        sd.emit("scalar", lambda s: s.activation(
            arena[0:1, LR:LR + T], arena[0:1, LV:LV + T], AF.Ln, bias=1e-5),
            waits=[(s_dv, VAR_D)], inc=s_ac)
        R_D = sd.emit("scalar", lambda s: s.activation(
            arena[0:1, LV:LV + T], arena[0:1, LR:LR + T], AF.Exp, scale=-0.5), inc=s_ac)
        bi, pt, w = bk.acquire()
        RB_D = None
        for tc in range(NTC):
            v = sd.emit("tensor", lambda t, pt=pt, tc=tc: t.matmul(
                pt[:, 0:TCW], ones_lhsT(), arena[0:1, LV + tc * TCW:LV + (tc + 1) * TCW],
                start=True, stop=True),
                waits=(w + [(s_ac, R_D)]) if tc == 0 else [(s_ac, RB_D)], inc=s_pe)
            RB_D = sd.emit("scalar", lambda s, pt=pt, tc=tc: s.copy(
                arena[:, LRB + tc * TCW:LRB + (tc + 1) * TCW], pt[:, 0:TCW]),
                waits=[(s_pe, v)], inc=s_ac)
        bk.release(bi, s_ac, RB_D)
        bi, pt, w = bk.acquire()
        MB_D = None
        for tc in range(NTC):
            v = sd.emit("tensor", lambda t, pt=pt, tc=tc: t.matmul(
                pt[:, 0:TCW], ones_lhsT(), arena[0:1, LM + tc * TCW:LM + (tc + 1) * TCW],
                start=True, stop=True),
                waits=(w if tc == 0 else [(s_ac, MB_D)]), inc=s_pe)
            MB_D = sd.emit("scalar", lambda s, pt=pt, tc=tc: s.copy(
                arena[:, LSQ1 + tc * TCW:LSQ1 + (tc + 1) * TCW], pt[:, 0:TCW]),
                waits=[(s_pe, v)], inc=s_ac)
        bk.release(bi, s_ac, MB_D)
        xr_rdy = []
        for c in range(NCH):
            sd.emit("vector", lambda v, c=c: v.tensor_sub(
                xr[:, c * T:(c + 1) * T], xb[:, c * T:(c + 1) * T],
                arena[:, LSQ1:LSQ1 + T]), waits=[(s_ac, MB_D), (s_ac, RB_D)], inc=s_dv)
            xr_rdy.append(sd.emit("vector", lambda v, c=c: v.tensor_mul(
                xr[:, c * T:(c + 1) * T], xr[:, c * T:(c + 1) * T],
                arena[:, LRB:LRB + T]), inc=s_dv))
        return xr_rdy

    def load_slabA(dram, row0, waits=()):
        i, view, w = poolA.acquire()
        rdy = gq.dma(gd(view, dram[row0:row0 + 128, :]), waits=list(w) + list(waits))
        return i, view, rdy

    def load_slabB(dram, row0, waits=()):
        i, view, w = poolB.acquire()
        rdy = gq.dma(gd(view, dram[row0:row0 + 128, :]), waits=list(w) + list(waits))
        return i, view, rdy

    def emit_attn(kind, XR_D):
        if kind == "a":
            wq, wk, wv = dwq_a, dwk_a, dwv_a
            bqc, bkc, bvr = bcaq, bca[:, 16:20], bvra
            ngrp, hpg = 4, 4
        else:
            wq, wk, wv = dwq_b, dwk_b, dwv_b
            bqc, bkc, bvr = bcbq, bcb[:, 16:32], bvrb
            ngrp, hpg = 16, 1
        base = list(CST_W)
        tail = [None]

        def proj_fm(view, wrdy, outcol, bcol, scale, extra):
            lastac = lastpe = None
            for tc in range(NTC):
                bi, pt, w = bk.acquire()
                for c in range(NCH):
                    lastpe = sd.emit("tensor", lambda t, pt=pt, view=view, c=c, tc=tc:
                                     t.matmul(
                        pt[:, 0:TCW], view[:, c * 128:(c + 1) * 128],
                        xr[:, c * T + tc * TCW:c * T + (tc + 1) * TCW],
                        start=(c == 0), stop=(c == NCH - 1)),
                        waits=([(s_dv, XR_D[c])] + ((w + [wrdy] + base + extra)
                               if c == 0 else [])),
                        inc=s_pe if c == NCH - 1 else None)
                lastac = sd.emit("scalar", lambda s, pt=pt, outcol=outcol, tc=tc,
                                 bcol=bcol, scale=scale: s.activation(
                    arena[:, outcol + tc * TCW:outcol + (tc + 1) * TCW], pt[:, 0:TCW],
                    AF.Identity, bias=bcol, scale=scale),
                    waits=[(s_pe, lastpe)], inc=s_ac)
                bk.release(bi, s_ac, lastac)
            return lastac, lastpe

        for g in range(ngrp):
            ki, kview, krdy = load_slabA(wk, g * 128)
            K_D, kpe = proj_fm(kview, krdy, AKK, bkc[:, g:g + 1], 1.0, [])
            poolA.release(ki, s_pe, kpe)
            vi, vview, vrdy = load_slabA(wv, g * 128)
            V_D = vpe = None
            for kt4 in range(4):
                bi, pt, w = bk.acquire()
                for sub in range(4):
                    kt = kt4 * 4 + sub
                    for c in range(NCH):
                        sd.emit("tensor",
                                lambda t, pt=pt, vw=vview, c=c, kt=kt, sub=sub: t.matmul(
                                    pt[:, sub * 128:(sub + 1) * 128],
                                    xr[:, c * T + kt * 128:c * T + kt * 128 + 128],
                                    vw[:, c * 128:(c + 1) * 128],
                                    start=(c == 0), stop=False),
                                waits=([(s_dv, XR_D[c])] + (w + [vrdy] + base
                                       if (sub == 0 and c == 0) else [])))
                    vpe = sd.emit("tensor", lambda t, pt=pt, sub=sub, g=g: t.matmul(
                        pt[:, sub * 128:(sub + 1) * 128], ones_lhsT(),
                        bvr[0:1, g * 128:g * 128 + 128], start=False, stop=True),
                        inc=s_pe if sub == 3 else None) or vpe
                V_D = sd.emit("scalar", lambda s, pt=pt, kt4=kt4: s.copy(
                    arena[:, AVV + kt4 * TCW:AVV + (kt4 + 1) * TCW], pt[:, :]),
                    waits=[(s_pe, vpe)], inc=s_ac)
                bk.release(bi, s_ac, V_D)
            poolA.release(vi, s_pe, vpe)

            for hh in range(hpg):
                h = g * hpg + hh
                qi, qview, qrdy = load_slabA(wq, h * 128)
                Q_D, qpe = proj_fm(qview, qrdy, AQ, bqc[:, h:h + 1], QSC, [])
                poolA.release(qi, s_pe, qpe)
                QKV = [(s_ac, max(Q_D, K_D, V_D))]

                def attn_finish(qc, obi, opt, RIv, h=None):
                    bi2, pt2, w2 = bk.acquire()
                    v2 = sd.emit("tensor", lambda t, pt2=pt2: t.matmul(
                        pt2[:, 0:TCW], ones_lhsT(), arena[0:1, ARS:ARS + TCW],
                        start=True, stop=True), waits=w2 + [(s_ac, RIv)], inc=s_pe)
                    crow_busy[0] = (s_pe, v2)
                    RSB = sd.emit("scalar", lambda s, pt2=pt2: s.copy(
                        arena[:, ARS:ARS + TCW], pt2[:, 0:TCW]),
                        waits=[(s_pe, v2)], inc=s_ac)
                    bk.release(bi2, s_ac, RSB)
                    if kind == "a":
                        sd.emit("vector", lambda v, opt=opt: v.tensor_mul(
                            fscr[:, 0:TCW], opt[:, 0:TCW], arena[:, ARS:ARS + TCW]),
                            waits=[(s_ac, RSB)], inc=s_dv)
                        xv = sd.emit("vector", lambda v, h=h, qc=qc: v.tensor_add(
                            xb[:, h * T + qc * TCW:h * T + (qc + 1) * TCW],
                            xb[:, h * T + qc * TCW:h * T + (qc + 1) * TCW],
                            fscr[:, 0:TCW]), inc=s_dv)
                        xfin[h] = (s_dv, xv)
                        bk.release(obi, s_dv, xv)
                    else:
                        aslot = AEV if (h * NTC + qc) % 2 == 0 else 11264
                        av = sd.emit("vector", lambda v, opt=opt, aslot=aslot: v.tensor_mul(
                            arena[:, aslot:aslot + TCW], opt[:, 0:TCW],
                            arena[:, ARS:ARS + TCW]),
                            waits=[(s_ac, RSB), aev_busy[(h * NTC + qc) % 2]], inc=s_dv)
                        bk.release(obi, s_dv, av)
                        aev_busy[(h * NTC + qc) % 2] = sq.dma(
                            lambda e, h=h, qc=qc, aslot=aslot: e.dma_start(
                                dattn[qc * 128:(qc + 1) * 128, h * TCW:(h + 1) * TCW],
                                arena[:, aslot:aslot + TCW]),
                            waits=[(s_dv, av)])

                for qc in range(NTC):
                    nkt = 4 * qc + 4
                    obi, opt, ow = bk.acquire()
                    rbi, rpt, rw = bk.acquire()
                    pt_free = [None, None, None, None, None]
                    PV_D = None
                    pend = []
                    half = [None]

                    def do_rspv(kt, dep, pslot, nkt=nkt, opt=opt, rpt=rpt, ow=ow, rw=rw):
                        pv = sd.emit("tensor", lambda t, opt=opt, pslot=pslot, kt=kt,
                                     nkt=nkt: t.matmul(
                            opt[:, 0:TCW], arena[:, AVV + kt * 128:AVV + kt * 128 + 128],
                            arena[:, pslot:pslot + TCW],
                            start=(kt == 0), stop=(kt == nkt - 1)),
                            waits=[dep] + (ow if kt == 0 else []), inc=s_pe)
                        if kt % 2 == 0:
                            half[0] = (pslot, dep)
                            return pv
                        p0, d0 = half[0]
                        sslot = (kt // 2 % 2) * TCW
                        sv = sd.emit("vector", lambda v, p0=p0, pslot=pslot, sslot=sslot:
                                     v.tensor_add(
                                         psum2[:, sslot:sslot + TCW],
                                         arena[:, p0:p0 + TCW],
                                         arena[:, pslot:pslot + TCW]),
                                     waits=[d0, dep], inc=s_dv)
                        return sd.emit("tensor", lambda t, rpt=rpt, sslot=sslot, kt=kt,
                                       nkt=nkt: t.matmul(
                            rpt[0:1, 0:TCW], onescol, psum2[:, sslot:sslot + TCW],
                            start=(kt == 1), stop=(kt == nkt - 1)),
                            waits=[(s_dv, sv)] + (rw if kt == 1 else []),
                            inc=s_pe)

                    for kt in range(nkt):
                        sbi, spt, sw = bk.acquire()
                        stv = sd.emit("tensor", lambda t, spt=spt, kt=kt, qc=qc: t.matmul(
                            spt[:, 0:TCW], arena[:, AKK + kt * 128:AKK + kt * 128 + 128],
                            arena[:, AQ + qc * TCW:AQ + (qc + 1) * TCW],
                            start=True, stop=True),
                            waits=sw + (QKV if kt == 0 else []), inc=s_pe)
                        pslot = (APT, APT + TCW, 10240, 10752, 11776)[kt % 5]
                        ev = sd.emit("scalar", lambda s, spt=spt, pslot=pslot: s.activation(
                            arena[:, pslot:pslot + TCW], spt[:, 0:TCW], AF.Exp, bias=EXPB),
                            waits=[(s_pe, stv), pt_free[kt % 5]], inc=s_ac)
                        bk.release(sbi, s_ac, ev)
                        dep = (s_ac, ev)
                        if kt >= 4 * qc:
                            j = kt - 4 * qc
                            mv = sd.emit("vector", lambda v, pslot=pslot, j=j: v.tensor_mul(
                                arena[:, pslot:pslot + TCW], arena[:, pslot:pslot + TCW],
                                cst[:, CST_MASK + j * TCW:CST_MASK + (j + 1) * TCW]),
                                waits=[(s_ac, ev)], inc=s_dv)
                            dep = (s_dv, mv)
                        pend.append((kt, dep, pslot))
                        if kt == min(5, nkt - 1) and tail[0] is not None:
                            tail[0]()
                            tail[0] = None
                        if len(pend) > 3:
                            k0, d0, p0 = pend.pop(0)
                            PV_D = do_rspv(k0, d0, p0)
                            if k0 % 2 == 1:
                                pt_free[k0 % 5] = (s_pe, PV_D)
                                pt_free[(k0 - 1) % 5] = (s_pe, PV_D)
                    while pend:
                        k0, d0, p0 = pend.pop(0)
                        PV_D = do_rspv(k0, d0, p0)
                        if k0 % 2 == 1:
                            pt_free[k0 % 5] = (s_pe, PV_D)
                            pt_free[(k0 - 1) % 5] = (s_pe, PV_D)
                    RIF = sd.emit("vector", lambda v, rpt=rpt: v.reciprocal(
                        fscr[0:1, TCW:2 * TCW], rpt[0:1, 0:TCW]),
                        waits=[(s_pe, PV_D), crow_busy[0], ri_busy[0]], inc=s_dv)
                    bk.release(rbi, s_dv, RIF)
                    RI = sd.emit("scalar", lambda s: s.copy(
                        arena[0:1, ARS:ARS + TCW], fscr[0:1, TCW:2 * TCW]),
                        waits=[(s_dv, RIF)], inc=s_ac)
                    ri_busy[0] = (s_ac, RI)
                    tail[0] = (lambda qc=qc, obi=obi, opt=opt, RIv=RI, h=h:
                               attn_finish(qc, obi, opt, RIv, h))
        if tail[0] is not None:
            tail[0]()
            tail[0] = None

        if kind == "b":
            AOUT_W = sq.barrier()
            for tc in range(NTC):
                COL = sq.dma(lambda e, tc=tc: e.dma_start(
                    arena[:, AWO:AWO + NH * TCW], dattn[tc * 128:(tc + 1) * 128, :]),
                    waits=AOUT_W + [(s_pe, s_pe.n), (s_dv, s_dv.n)])
                for dt in range(NCH):
                    wi, wview, wrdy = load_slabA(dwo_b, dt * 128)
                    bi, pt, w = bk.acquire()
                    v = None
                    for c in range(NCH):
                        v = sd.emit("tensor", lambda t, pt=pt, wv=wview, c=c: t.matmul(
                            pt[:, 0:TCW], wv[:, c * 128:(c + 1) * 128],
                            arena[:, AWO + c * TCW:AWO + (c + 1) * TCW],
                            start=(c == 0), stop=(c == NCH - 1)),
                            waits=(w + [wrdy, COL]
                                   if c == 0 else []),
                            inc=s_pe if c == NCH - 1 else None) or v
                    poolA.release(wi, s_pe, v)
                    xv = sd.emit("vector", lambda vv, pt=pt, dt=dt, tc=tc:
                                 vv.scalar_tensor_tensor(
                                     xb[:, dt * T + tc * TCW:dt * T + (tc + 1) * TCW],
                                     pt[:, 0:TCW],
                                     bcb[:, 32 + dt:32 + dt + 1],
                                     xb[:, dt * T + tc * TCW:dt * T + (tc + 1) * TCW],
                                     ALU.add, ALU.add),
                                 waits=[(s_pe, v)], inc=s_dv)
                    xfin[dt] = (s_dv, xv)
                    bk.release(bi, s_dv, xv)

    def emit_gating_units(XR_D, box):
        base = [(s_ac, GATE_RDY)] + CST_W

        def unit(tt):
            bi, pt, w = bk.acquire()
            v = None
            for c in range(NCH):
                v = sd.emit("tensor", lambda t, pt=pt, c=c, tt=tt: t.matmul(
                    pt[:, 0:NE], xr[:, c * T + tt * 128:c * T + tt * 128 + 128],
                    gwb[:, c * NE:(c + 1) * NE], start=(c == 0), stop=(c == NCH - 1)),
                    waits=[(s_dv, XR_D[c])] + ((w + base) if c == 0 else []),
                    inc=s_pe if c == NCH - 1 else None) or v
            nm = sd.emit("vector", lambda vv, pt=pt: vv.reduce_max(
                gsc[:, 84:85], pt[:, 0:NE], AX, negate=True), waits=[(s_pe, v)], inc=s_dv)
            ex = sd.emit("scalar", lambda s, pt=pt: s.activation(
                gsc[:, 0:NE], pt[:, 0:NE], AF.Exp, bias=gsc[:, 84:85]),
                waits=[(s_dv, nm)], inc=s_ac)
            bk.release(bi, s_ac, ex)
            for fn in (
                lambda vv: vv.reduce_sum(gsc[:, 82:83], gsc[:, 0:NE], AX),
                lambda vv: vv.reciprocal(gsc[:, 83:84], gsc[:, 82:83]),
                lambda vv: vv.tensor_scalar_mul(gsc[:, 0:NE], gsc[:, 0:NE], gsc[:, 83:84]),
                lambda vv: vv.tensor_add(gsc[:, 16:32], gsc[:, 0:NE], gbb[:, :]),
                lambda vv: vv.tensor_copy(gsc[:, 32:48], gsc[:, 16:32]),
            ):
                sd.emit("vector", fn, waits=[(s_ac, ex)], inc=s_dv)
            for _ in range(5):
                sd.emit("vector", lambda vv: vv.reduce_max(gsc[:, 80:81], gsc[:, 32:48], AX),
                        inc=s_dv)
                sd.emit("vector", lambda vv: vv.tensor_scalar(
                    gsc[:, 48:64], gsc[:, 32:48], gsc[:, 80:81], None, ALU.is_equal),
                    inc=s_dv)
                sd.emit("vector", lambda vv: vv.scalar_tensor_tensor(
                    gsc[:, 32:48], gsc[:, 48:64], -1e30, gsc[:, 32:48], ALU.mult, ALU.add),
                    inc=s_dv)
            sd.emit("vector", lambda vv: vv.reduce_max(gsc[:, 81:82], gsc[:, 32:48], AX),
                    inc=s_dv)
            sd.emit("vector", lambda vv: vv.tensor_scalar(
                gsc[:, 48:64], gsc[:, 16:32], gsc[:, 81:82], None, ALU.is_ge), inc=s_dv)
            CW = sd.emit("vector", lambda vv: vv.tensor_mul(
                gsc[:, 64:80], gsc[:, 0:NE], gsc[:, 48:64]), inc=s_dv)
            C16 = sd.emit("scalar", lambda s: s.copy(cwf[:, :], gsc[:, 64:80]),
                          waits=[(s_dv, CW)], inc=s_ac)
            po = (tt % 8) * 128
            trv = sd.emit("tensor", lambda t, po=po: t.transpose(
                ps16[0:NE, po:po + 128], cwf[:, :], cst[:, CST_IDEN:CST_IDEN + 128]),
                waits=[(s_ac, C16), cwtr_rel[tt % 8]], inc=s_pe)
            box[0] = sd.emit("scalar", lambda s, po=po, tt=tt: s.copy(
                cwT[:, tt * 128:(tt + 1) * 128], ps16[0:NE, po:po + 128]),
                waits=[(s_pe, trv)], inc=s_ac)
            cwtr_rel[tt % 8] = (s_ac, box[0])
        return [(lambda tt=tt: unit(tt)) for tt in range(NCH)]

    def emit_ffn(first, XR_D):
        cwt_box = [None]
        gunits = emit_gating_units(XR_D, cwt_box)
        base = list(CST_W)
        for e in [NE] + list(range(NE)):
            shared = (e == NE)
            nht = NSHT if shared else NHT
            w1d, w3d, w2d = (dws1, dws3, dws2) if shared else (dw1, dw3, dw2)
            rbase = 0 if shared else e * NHT
            b1cv = (lambda ht: bcs[:, ht:ht + 1]) if shared else \
                (lambda ht, e=e: bc1[:, e * NHT + ht:e * NHT + ht + 1])
            b3cv = (lambda ht: bcs[:, NSHT + ht:NSHT + ht + 1]) if shared else \
                (lambda ht, e=e: bc3[:, e * NHT + ht:e * NHT + ht + 1])
            C_RDY = None
            if not shared:
                while gunits:
                    gunits.pop(0)()
                if e in Cpend:
                    CR = Cpend.pop(e)
                else:
                    cwaits = [(s_ac, cwt_box[0]), crow_busy[0]]
                    if C_cons[0]:
                        cwaits.append((s_dv, C_cons[0]))
                    CR = sq.dma(lambda s2, e=e: s2.dma_start(
                        arena[0:1, CB:CB + T], cwT[e:e + 1, :]), waits=cwaits)
                bi, pt, w = bk.acquire()
                lastpe = None
                C_RDY = None
                for tc in range(NTC):
                    lastpe = sd.emit("tensor", lambda t, pt=pt, tc=tc: t.matmul(
                        pt[:, 0:TCW], ones_lhsT(),
                        arena[0:1, CB + tc * TCW:CB + (tc + 1) * TCW],
                        start=True, stop=True),
                        waits=(w + [CR]) if tc == 0 else [(s_ac, C_RDY)], inc=s_pe)
                    C_RDY = sd.emit("scalar", lambda s, pt=pt, tc=tc: s.copy(
                        arena[:, CB + tc * TCW:CB + (tc + 1) * TCW], pt[:, 0:TCW]),
                        waits=[(s_pe, lastpe)], inc=s_ac)
                crow_busy[0] = (s_pe, lastpe)
                bk.release(bi, s_ac, C_RDY)

            groups = ([list(range(0, 6)), list(range(6, 11))] if not shared else
                      [list(range(0, 6)), list(range(6, 12)), list(range(12, 18)),
                       list(range(18, 22))])
            for gi, grp in enumerate(groups):
                g0 = grp[0]
                H_D = None
                a1pe = a3pe = None
                for ht in grp:
                    if shared and gunits:
                        gunits.pop(0)()
                        if gunits:
                            gunits.pop(0)()
                    i1, v1, r1 = load_slabA(w1d, (rbase + ht) * 128)
                    i3, v3, r3 = load_slabA(w3d, (rbase + ht) * 128)
                    for tc in range(NTC):
                        b1i, p1, w1w = bk.acquire()
                        b3i, p3, w3w = bk.acquire()
                        a1pe = a3pe = None
                        for c in range(NCH):
                            a1pe = sd.emit("tensor", lambda t, p1=p1, v1=v1, c=c, tc=tc:
                                           t.matmul(
                                p1[:, 0:TCW], v1[:, c * 128:(c + 1) * 128],
                                xr[:, c * T + tc * TCW:c * T + (tc + 1) * TCW],
                                start=(c == 0), stop=(c == NCH - 1)),
                                waits=[(s_dv, XR_D[c])] + ((w1w + [r1] + base)
                                                          if c == 0 else []),
                                inc=s_pe if c == NCH - 1 else None) or a1pe
                        for c in range(NCH):
                            a3pe = sd.emit("tensor", lambda t, p3=p3, v3=v3, c=c, tc=tc:
                                           t.matmul(
                                p3[:, 0:TCW], v3[:, c * 128:(c + 1) * 128],
                                xr[:, c * T + tc * TCW:c * T + (tc + 1) * TCW],
                                start=(c == 0), stop=(c == NCH - 1)),
                                waits=[(s_dv, XR_D[c])] + ((w3w + [r3])
                                                          if c == 0 else []),
                                inc=s_pe if c == NCH - 1 else None) or a3pe
                        sfx = (tc % 2) * TCW
                        b1a = b1cv(ht)
                        b3a = b3cv(ht)
                        sv = sd.emit("scalar", lambda s, p1=p1, sfx=sfx, b1a=b1a:
                                     s.activation(
                            fscr[:, sfx:sfx + TCW], p1[:, 0:TCW], AF.Silu, bias=b1a),
                            waits=[(s_pe, a1pe), s1_busy[tc % 2]], inc=s_ac)
                        bk.release(b1i, s_ac, sv)
                        hcol = HBC + (ht - g0) * T + tc * TCW
                        if shared:
                            hv = sd.emit("vector", lambda vv, p3=p3, sfx=sfx, hcol=hcol,
                                         b3a=b3a: vv.scalar_tensor_tensor(
                                             arena[:, hcol:hcol + TCW], p3[:, 0:TCW],
                                             b3a, fscr[:, sfx:sfx + TCW],
                                             ALU.add, ALU.mult),
                                         waits=[(s_pe, a3pe), (s_ac, sv)], inc=s_dv)
                        else:
                            sd.emit("vector", lambda vv, p3=p3, sfx=sfx, b3a=b3a:
                                    vv.scalar_tensor_tensor(
                                        fscr[:, sfx:sfx + TCW], p3[:, 0:TCW],
                                        b3a, fscr[:, sfx:sfx + TCW],
                                        ALU.add, ALU.mult),
                                    waits=[(s_pe, a3pe), (s_ac, sv), (s_ac, C_RDY)],
                                    inc=s_dv)
                            hv = sd.emit("vector", lambda vv, sfx=sfx, hcol=hcol, tc=tc:
                                         vv.tensor_mul(
                                             arena[:, hcol:hcol + TCW],
                                             fscr[:, sfx:sfx + TCW],
                                             arena[:, CB + tc * TCW:CB + (tc + 1) * TCW]),
                                         inc=s_dv)
                        s1_busy[tc % 2] = (s_dv, hv)
                        bk.release(b3i, s_dv, hv)
                        H_D = hv
                    poolA.release(i1, s_pe, a1pe)
                    poolA.release(i3, s_pe, a3pe)
                nxt = 0 if shared else e + 1
                if gi == len(groups) - 1 and nxt < NE:
                    Cpend[nxt] = sq.dma(
                        lambda s2, e1=nxt: s2.dma_start(
                            arena[0:1, CB:CB + T], cwT[e1:e1 + 1, :]),
                        waits=[(s_ac, cwt_box[0]), crow_busy[0], (s_dv, s_dv.n)])
                for dq in range(8):
                    slabs = []
                    for hc in grp:
                        slabs.append(load_slabB(w2d, ((rbase + hc) * 8 + dq) * 128))
                    lastpe = None
                    for j in range(2):
                        dt = dq * 2 + j
                        for tc in range(NTC):
                            bi, pt, w = bk.acquire()
                            nacc = len(grp) + (1 if (e == 0 and gi == 0) else 0)
                            k = 0
                            for hi, (ib, vb, rb) in enumerate(slabs):
                                hcol = HBC + hi * T + tc * TCW
                                sd.emit("tensor",
                                        lambda t, pt=pt, vb=vb, j=j, hcol=hcol,
                                        st=(k == 0), sp=(k == nacc - 1): t.matmul(
                                            pt[:, 0:TCW], vb[:, j * 128:(j + 1) * 128],
                                            arena[:, hcol:hcol + TCW], start=st, stop=sp),
                                        waits=(w + [rb, (s_dv, H_D)]
                                               if k == 0 else [rb]),
                                        inc=s_pe if k == nacc - 1 else None)
                                k += 1
                            if e == 0 and gi == 0:
                                sd.emit("tensor", lambda t, pt=pt, dt=dt, tc=tc,
                                        sp=(k == nacc - 1): t.matmul(
                                            pt[:, 0:TCW], b2b[:, dt * 128:(dt + 1) * 128],
                                            cwT[:, tc * TCW:(tc + 1) * TCW],
                                            start=False, stop=sp),
                                        waits=[(s_ac, cwt_box[0])],
                                        inc=s_pe if k == nacc - 1 else None)
                                k += 1
                            lastpe = s_pe.n
                            if shared and gi == 0:
                                bsc = bcs[:, 2 * NSHT + dt:2 * NSHT + dt + 1]
                                op0 = ALU.add
                            else:
                                bsc = 1.0
                                op0 = ALU.mult
                            xv = sd.emit("vector", lambda vv, pt=pt, dt=dt, tc=tc,
                                         bsc=bsc, op0=op0: vv.scalar_tensor_tensor(
                                             xb[:, dt * T + tc * TCW:dt * T + (tc + 1) * TCW],
                                             pt[:, 0:TCW], bsc,
                                             xb[:, dt * T + tc * TCW:dt * T + (tc + 1) * TCW],
                                             op0, ALU.add),
                                         waits=[(s_pe, lastpe)], inc=s_dv)
                            xfin[dt] = (s_dv, xv)
                            bk.release(bi, s_dv, xv)
                    for (ib, vb, rb) in slabs:
                        poolB.release(ib, s_pe, lastpe)
                C_cons[0] = s_dv.n

    kinds = ["attn_a", "ffn1", "attn_b", "ffn2"]
    for si in range(n_sub):
        XR_D = emit_ln()
        k = kinds[si]
        if k == "attn_a":
            emit_attn("a", XR_D)
        elif k == "attn_b":
            emit_attn("b", XR_D)
        else:
            emit_ffn(k == "ffn1", XR_D)

    fin = [(s_pe, s_pe.n), (s_dv, s_dv.n), (s_ac, s_ac.n)]
    st_busy = [None, None]
    for c in range(NCH):
        for piece in range(4):
            sl = (piece % 2) * TCW
            av = sd.emit("scalar", lambda s, c=c, sl=sl, piece=piece: s.copy(
                fscr[:, sl:sl + TCW],
                xb[:, c * T + piece * TCW:c * T + (piece + 1) * TCW]),
                waits=fin + [st_busy[piece % 2]], inc=s_ac)
            st_busy[piece % 2] = sq.dma(
                lambda e, c=c, sl=sl, piece=piece: e.dma_start(
                    dout[c * 128:(c + 1) * 128, piece * TCW:(piece + 1) * TCW],
                    fscr[:, sl:sl + TCW]),
                waits=[(s_ac, av)])
    fin_waits = sq.barrier() + gq.barrier()

    def final_wait(e, fw=tuple(fin_waits)):
        ins = None
        for sem, val in fw:
            ins = e.wait_ge(sem.h, val)
        return ins
    sd.emit("sync", final_wait)

    with nc.Block() as block:
        sd.drain(block)
    return nc


def _swz_proj(w):
    return np.ascontiguousarray(
        w.reshape(16, 128, 16, 128).transpose(2, 1, 0, 3).reshape(2048, 2048))


def _host_arrays(inputs):
    f = np.float32

    def c(x):
        return np.ascontiguousarray(x).astype(f)

    a = {}
    wq = np.asarray(inputs["L0_attn_a_Wq"])
    a["wq_a"] = c(wq.reshape(16, 16, 128, 128).transpose(0, 2, 1, 3).reshape(2048, 2048))
    a["wk_a"] = c(np.asarray(inputs["L0_attn_a_Wk"]).reshape(4, 16, 128, 128)
                  .transpose(0, 2, 1, 3).reshape(512, 2048))
    a["wv_a"] = c(np.asarray(inputs["L0_attn_a_Wv"]).reshape(4, 16, 128, 128)
                  .transpose(0, 2, 1, 3).reshape(512, 2048))
    a["bq_a"] = c(np.asarray(inputs["L0_attn_a_bq"]).reshape(1, -1))
    a["bk_a"] = c(np.asarray(inputs["L0_attn_a_bk"]).reshape(1, -1))
    a["bv_a"] = c(np.asarray(inputs["L0_attn_a_bv"]).reshape(1, -1))
    a["wq_b"] = c(_swz_proj(np.asarray(inputs["L0_attn_b_w_q"])))
    a["wk_b"] = c(_swz_proj(np.asarray(inputs["L0_attn_b_w_k"])))
    a["wv_b"] = c(_swz_proj(np.asarray(inputs["L0_attn_b_w_v"])))
    a["wo_b"] = c(_swz_proj(np.asarray(inputs["L0_attn_b_w_o"])))
    a["bq_b"] = c(np.asarray(inputs["L0_attn_b_b_q"]).reshape(1, -1))
    a["bk_b"] = c(np.asarray(inputs["L0_attn_b_b_k"]).reshape(1, -1))
    a["bv_b"] = c(np.asarray(inputs["L0_attn_b_b_v"]).reshape(1, -1))
    a["bo_b"] = c(np.asarray(inputs["L0_attn_b_b_o"]).reshape(1, -1))
    a["gw"] = c(np.asarray(inputs["L0_ffn_gate_w"]).T.reshape(16, 128, 16)
                .transpose(1, 0, 2).reshape(128, 256))
    a["gb"] = c(np.asarray(inputs["L0_ffn_gate_b"]).reshape(1, -1))
    a["w1"] = c(np.asarray(inputs["L0_ffn_w1"]).transpose(0, 2, 1)
                .reshape(16, 16, 128, 11, 128).transpose(0, 3, 2, 1, 4)
                .reshape(16 * 11 * 128, 2048))
    a["w3"] = c(np.asarray(inputs["L0_ffn_w3"]).transpose(0, 2, 1)
                .reshape(16, 16, 128, 11, 128).transpose(0, 3, 2, 1, 4)
                .reshape(16 * 11 * 128, 2048))
    a["w2"] = c(np.asarray(inputs["L0_ffn_w2"]).transpose(0, 2, 1)
                .reshape(16, 11, 128, 8, 2, 128).transpose(0, 1, 3, 2, 4, 5)
                .reshape(16 * 11 * 8 * 128, 256))
    a["b1"] = c(np.asarray(inputs["L0_ffn_b1"]))
    a["b3"] = c(np.asarray(inputs["L0_ffn_b3"]))
    a["b2"] = c(np.asarray(inputs["L0_ffn_b2"]))
    a["ws1"] = c(np.asarray(inputs["L0_ffn_ws1"]).T.reshape(16, 128, 22, 128)
                 .transpose(2, 1, 0, 3).reshape(22 * 128, 2048))
    a["ws3"] = c(np.asarray(inputs["L0_ffn_ws3"]).T.reshape(16, 128, 22, 128)
                 .transpose(2, 1, 0, 3).reshape(22 * 128, 2048))
    a["ws2"] = c(np.asarray(inputs["L0_ffn_ws2"]).T.reshape(22, 128, 8, 2, 128)
                 .transpose(0, 2, 1, 3, 4).reshape(22 * 8 * 128, 256))
    a["bs1"] = c(np.asarray(inputs["L0_ffn_bs1"]).reshape(1, -1))
    a["bs3"] = c(np.asarray(inputs["L0_ffn_bs3"]).reshape(1, -1))
    a["bs2"] = c(np.asarray(inputs["L0_ffn_bs2"]).reshape(1, -1))
    cstv = np.zeros((128, 2176), dtype=np.float16)
    p = np.arange(128)[:, None]
    fc = np.arange(512)[None, :]
    for j in range(4):
        cstv[:, j * 512:(j + 1) * 512] = (fc >= (p + 128 * j)).astype(np.float16)
    cstv[:, 2048:2176] = np.eye(128, dtype=np.float16)
    a["cst"] = cstv
    a["cfl"] = np.tile(np.array([[1e-5, -4.0]], dtype=np.float32), (128, 1))
    b1 = np.asarray(inputs["L0_ffn_b1"])
    b3 = np.asarray(inputs["L0_ffn_b3"])
    a["bc1"] = c(b1.reshape(16, 11, 128).transpose(2, 0, 1).reshape(128, 176))
    a["bc3"] = c(b3.reshape(16, 11, 128).transpose(2, 0, 1).reshape(128, 176))
    a["bcs"] = c(np.concatenate([
        np.asarray(inputs["L0_ffn_bs1"]).reshape(22, 128).T,
        np.asarray(inputs["L0_ffn_bs3"]).reshape(22, 128).T,
        np.asarray(inputs["L0_ffn_bs2"]).reshape(16, 128).T], axis=1))
    a["bca"] = c(np.concatenate([
        np.asarray(inputs["L0_attn_a_bq"]).T,
        np.asarray(inputs["L0_attn_a_bk"]).T], axis=1))
    a["bcb"] = c(np.concatenate([
        np.asarray(inputs["L0_attn_b_b_q"]).reshape(16, 128).T,
        np.asarray(inputs["L0_attn_b_b_k"]).reshape(16, 128).T,
        np.asarray(inputs["L0_attn_b_b_o"]).reshape(16, 128).T], axis=1))
    return a


def _swz_x(xbat):
    return np.ascontiguousarray(
        xbat.T.reshape(16, 128, 2048).transpose(1, 0, 2).reshape(128, 16 * 2048)
    ).astype(np.float32)


def kernel(**inputs):
    shared = _host_arrays(inputs)
    x = np.asarray(inputs["L0_attn_a_x"])
    nc = build_program(4)
    in_maps = []
    for b in range(8):
        m = dict(shared)
        m["xin"] = _swz_x(np.asarray(x[b], dtype=np.float32))
        in_maps.append(m)
    res = run_bass_kernel_spmd(nc, in_maps, core_ids=list(range(8)))
    out = np.empty((8, 2048, 2048), dtype=np.float32)
    for b in range(8):
        out[b] = np.asarray(res.results[b]["out"]).reshape(2048, 2048).T
    return out
